# version 23
# baseline (speedup 1.0000x reference)
import numpy as np
import ml_dtypes
from contextlib import ExitStack

import concourse.bass as bass
import concourse.mybir as mybir
from concourse.bass_utils import run_bass_kernel_spmd
from concourse.alu_op_type import AluOpType as OP

F32 = mybir.dt.float32
BF16 = mybir.dt.bfloat16
AF = mybir.ActivationFunctionType
AX = mybir.AxisListType.X

T, C, H, DH, DF = 2048, 1024, 16, 64, 4096
NT = T // 128
NCH = C // 128
NJC = DF // 128
NHP = H // 2
QB = 512
NQB = T // QB
TBS = 512
NTB = T // TBS
M32 = 12582912.0
M64 = 6755399441055744.0
EPS = 1e-5
B0 = 20.0
NMASK = -1.0e9
W1CH = 32
W2CH = 32
AMX_OFF = (640, 768, 1664, 1792)


def _build():
    nc = bass.Bass("TRN2", target_bir_lowering=False, debug=False)

    def din(name, shape, dt=F32):
        return nc.dram_tensor(name, list(shape), dt, kind="ExternalInput").ap()

    x_d = din("x", (T, C))
    wq_d = din("wq_t", (8, 128, 1024)); wk_d = din("wk_t", (8, 128, 1024))
    wv_d = din("wv_t", (4, 128, 2048)); wo_d = din("wo_t", (C, C))
    w1_d = din("w1_t", (C, DF)); w2_d = din("w2_t", (DF, C))
    cw_d = din("cw", (128, 96)); cb_d = din("cb", (128, 32))
    bq_d = din("bq", (128, 8)); bk_d = din("bk", (128, 8))
    bv_d = din("bv", (1, C)); bo_d = din("bo", (1, C))
    idf_d = din("identf", (128, 128))
    idb_d = din("identb", (128, 128), BF16)
    ones_d = din("onesr", (1, 128))

    out_d = nc.dram_tensor("out", [T, C], F32, kind="ExternalOutput").ap()
    w1q_d = nc.dram_tensor("w1q_stage", [NJC, 128, 1024], BF16, kind="Internal").ap()
    w2q_d = nc.dram_tensor("w2q_stage", [DF, C], BF16, kind="Internal").ap()

    st = ExitStack()
    E = st.enter_context

    semn, cnt = {}, {}
    def sem(name):
        if name not in semn:
            semn[name] = E(nc.semaphore(name))
            cnt[name] = 0
        return semn[name]
    def inc(name, by=1):
        sem(name); cnt[name] += by
        return cnt[name]
    def val(name):
        sem(name); return cnt[name]

    identf = E(nc.sbuf_tensor("identf_s", [128, 128], F32))
    onesr = E(nc.sbuf_tensor("onesr_s", [1, 128], F32))
    onesc = E(nc.sbuf_tensor("onesc", [128, 1], F32))
    bqc = E(nc.sbuf_tensor("bqc", [128, 8], F32))
    bkc = E(nc.sbuf_tensor("bkc", [128, 8], F32))
    cwb = E(nc.sbuf_tensor("cwb", [128, 96], F32))
    cws = E(nc.sbuf_tensor("cws", [128, 96], F32))
    cbc = E(nc.sbuf_tensor("cbc", [128, 32], F32))
    meff = E(nc.sbuf_tensor("meff", [128, 1], F32))
    meffneg = E(nc.sbuf_tensor("meffneg", [128, 1], F32))
    mask = E(nc.sbuf_tensor("mask", [128, 16], F32))
    bias_exp = E(nc.sbuf_tensor("bias_exp", [128, 16], F32))
    invsx = E(nc.sbuf_tensor("invsx", [128, 16], F32))
    shc = E(nc.sbuf_tensor("shc", [128, 16], F32))
    sh_cols = E(nc.sbuf_tensor("sh_cols", [128, 16], F32))
    wsum1 = E(nc.sbuf_tensor("wsum1", [128, W1CH], F32))
    wsum2 = E(nc.sbuf_tensor("wsum2", [128, W2CH], F32))
    sw1c = E(nc.sbuf_tensor("sw1c", [128, 1], F32))
    sw2c = E(nc.sbuf_tensor("sw2c", [128, 1], F32))
    mw2c = E(nc.sbuf_tensor("mw2c", [128, 1], F32))
    tiny = E(nc.sbuf_tensor("tiny", [128, 16], F32))
    trow = E(nc.sbuf_tensor("trow", [1, 512], F32))
    xqT = E(nc.sbuf_tensor("xqT", [128, NCH * 2050], BF16))
    xqTv = xqT.ap().rearrange("p (c t) -> p c t", c=NCH)
    g_bor = nc.sbuf_tensor("bor", [1, 1024], F32); bor = g_bor.__enter__()
    g_onesb = nc.sbuf_tensor("onesb", [1, 128], BF16); onesb = g_onesb.__enter__()
    g_bvb = nc.sbuf_tensor("bvb", [1, 1024], BF16); bvb = g_bvb.__enter__()
    g_ctx = nc.sbuf_tensor("ctx", [128, NCH * 2048], BF16); ctx = g_ctx.__enter__()
    ctxv = ctx.ap().rearrange("p (c t) -> p c t", c=NCH)
    g_dr = nc.sbuf_tensor("denrow", [1, 1024], F32); denrow = g_dr.__enter__()
    g_rr = nc.sbuf_tensor("rtile", [128, 512], F32); rtile = g_rr.__enter__()

    def w1_src(ch):
        r, h = ch // 4, ch % 4
        return w1_d[r * 128:(r + 1) * 128, h * 1024:(h + 1) * 1024]

    def w2_src(ch):
        return w2_d[ch * 128:(ch + 1) * 128, :]

    psA = E(nc.psum_tensor("psA", [128, 2048], F32))
    psC = E(nc.psum_tensor("psC", [128, 1536], F32))
    g_psB = nc.psum_tensor("psB", [128, 1024], BF16)
    psB = g_psB.__enter__()

    blk = E(nc.Block())
    def on(engname, fn):
        getattr(blk, engname)(fn)

    def p0_sync(q):
        for dst, src in [
            (identf[:], idf_d[:]), (onesr[:], ones_d[:]),
            (bqc[:], bq_d[:]), (bkc[:], bk_d[:]),
            (bor[:], bo_d[:]),
            (cwb[:], cw_d[:]), (cbc[:], cb_d[:]),
        ]:
            q.dma_start(out=dst, in_=src).then_inc(sem("cst"), 16)
            inc("cst", 16)
    on("sync", p0_sync)
    CST = val("cst")

    def p0_vec(v):
        v.wait_ge(sem("cst"), CST)
        v.memset(onesb[:], 1.0).then_inc(sem("p0v"), 1); inc("p0v")
        v.memset(onesc[:], 1.0).then_inc(sem("p0v"), 1); inc("p0v")
        v.wait_ge(sem("p0v"), val("p0v"))
        v.tensor_scalar(tiny[:, 0:1], onesc[:], 2.0 ** -30, 1.0,
                        OP.add, OP.subtract).then_inc(sem("p0v"), 1); inc("p0v")
        v.wait_ge(sem("p0v"), val("p0v"))
        v.tensor_scalar(tiny[:, 1:2], tiny[:, 0:1], 2.0 ** 30, None,
                        OP.mult).then_inc(sem("p0v"), 1); inc("p0v")
        v.wait_ge(sem("p0v"), val("p0v"))
        v.tensor_scalar(meff[:], tiny[:, 1:2], M64, M32,
                        OP.mult, OP.add).then_inc(sem("p0v"), 1); inc("p0v")
        v.wait_ge(sem("p0v"), val("p0v"))
        v.tensor_scalar(meffneg[:], meff[:], -1.0, None,
                        OP.mult).then_inc(sem("p0v"), 1); inc("p0v")
    on("vector", p0_vec)
    MEFF = val("p0v")

    g_xnT = nc.sbuf_tensor("xnT", [128, NCH * 2048], BF16)
    xnT = g_xnT.__enter__()
    xnTv = xnT.ap().rearrange("p (c t) -> p c t", c=NCH)
    g_wf = nc.sbuf_tensor("wf", [128, 4 * 1024], F32)
    wf = g_wf.__enter__()
    g_x_t = nc.sbuf_tensor("x_t", [128, 2 * 1024], F32); x_t = g_x_t.__enter__()
    g_xn_t = nc.sbuf_tensor("xn_t", [128, 2 * 1024], F32); xn_t = g_xn_t.__enter__()
    g_xnb = nc.sbuf_tensor("xnb_t", [128, 2 * 1024], BF16); xnb_t = g_xnb.__enter__()
    g_xqb = nc.sbuf_tensor("xqb_t", [128, 2 * 1024], BF16); xqb_t = g_xqb.__enter__()
    g_stt = nc.sbuf_tensor("stt", [128, 2 * 16], F32); stt = g_stt.__enter__()
    g_identb = nc.sbuf_tensor("identb_s", [128, 128], BF16)
    identb = g_identb.__enter__()

    def p1_sync(q):
        q.dma_start(out=identb[:], in_=idb_d[:]).then_inc(sem("cstb"), 16)
        inc("cstb", 16)
        for i in range(NT):
            b = i % 2
            if i >= 2:
                q.wait_ge(sem(f"xf{b}"), i // 2)
            q.dma_start(out=x_t[:, b * 1024:(b + 1) * 1024],
                        in_=x_d[i * 128:(i + 1) * 128, :]).then_inc(sem(f"xl{b}"), 16)
            inc(f"xl{b}", 16)
    on("sync", p1_sync)

    def p1_tensor(t):
        t.wait_ge(sem("cst"), CST)
        t.wait_ge(sem("cstb"), 16)
        for i in range(NT):
            b = i % 2
            for g, semname in enumerate(["p1xb", "p1xq"]):
                srcbuf = xnb_t if g == 0 else xqb_t
                t.wait_ge(sem(semname), i + 1)
                if 2 * i + g >= 1:
                    t.wait_ge(sem("p1ev"), 2 * i + g)
                for j in range(8):
                    mm = t.transpose(psB[:, j * 128:(j + 1) * 128],
                                     srcbuf[:, b * 1024 + j * 128:b * 1024 + (j + 1) * 128],
                                     identb[:])
                mm.then_inc(sem("p1tpm"), 1)
                inc("p1tpm")
    on("tensor", p1_tensor)

    def p1_scalar(s):
        NW = W1CH + W2CH

        def wdma(ch):
            if ch >= 2:
                s.wait_ge(sem("wabs"), ch - 1)
            src = w1_src(ch) if ch < W1CH else w2_src(ch - W1CH)
            s.dma_start(out=wf[:, (ch % 3) * 1024:(ch % 3 + 1) * 1024],
                        in_=src).then_inc(sem(f"wl{ch % 3}"), 16)
            inc(f"wl{ch % 3}", 16)

        def wabs(ch):
            acc = wsum1[:, ch:ch + 1] if ch < W1CH else wsum2[:, ch - W1CH:ch - W1CH + 1]
            s.wait_ge(sem(f"wl{ch % 3}"), 16 * (ch // 3 + 1))
            s.activation(out=wf[:, (ch % 3) * 1024:(ch % 3 + 1) * 1024],
                         in_=wf[:, (ch % 3) * 1024:(ch % 3 + 1) * 1024],
                         func=AF.Abs, accum_out=acc).then_inc(sem("wabs"), 1)
            inc("wabs")

        def quant_tail(i2):
            b2 = i2 % 2
            xnt2 = xn_t[:, b2 * 1024:(b2 + 1) * 1024]
            s.wait_ge(sem("p1q1"), i2 + 1)
            s.activation(out=xnt2, in_=xnt2, func=AF.Identity,
                         bias=meffneg[:, 0:1]).then_inc(sem("p1rt"), 1)
            inc("p1rt")
            s.wait_ge(sem("p1rt"), val("p1rt"))
            if i2 >= 2:
                s.wait_ge(sem("p1tpm"), 2 * i2 - 2)
            s.activation(out=xqb_t[:, b2 * 1024:(b2 + 1) * 1024], in_=xnt2,
                         func=AF.Identity,
                         scale=invsx[:, i2:i2 + 1]).then_inc(sem("p1xq"), 1)
            inc("p1xq")

        wdma(0); wdma(1)
        s.wait_ge(sem("p0v"), MEFF)
        for i in range(NT):
            b = i % 2
            s.wait_ge(sem(f"xl{b}"), 16 * (i // 2 + 1))
            if i >= 1:
                s.wait_ge(sem("p1sq"), i)
            s.activation(out=psA[:, 0:1024], in_=x_t[:, b * 1024:(b + 1) * 1024],
                         func=AF.Square,
                         accum_out=stt[:, b * 16 + 0:b * 16 + 1]).then_inc(sem("p1sq"), 1)
            inc("p1sq")
            for ch in (4 * i, 4 * i + 1):
                wabs(ch)
                if ch + 2 < NW:
                    wdma(ch + 2)
            if i >= 1:
                quant_tail(i - 1)
            s.wait_ge(sem("p1v"), 16 * i + 5)
            s.activation(out=stt[:, b * 16 + 3:b * 16 + 4],
                         in_=stt[:, b * 16 + 2:b * 16 + 3],
                         func=AF.Sqrt).then_inc(sem("p1sqrt"), 1)
            inc("p1sqrt")
            for ch in (4 * i + 2, 4 * i + 3):
                wabs(ch)
                if ch + 2 < NW:
                    wdma(ch + 2)
        quant_tail(NT - 1)
    on("scalar", p1_scalar)

    def p1_vector(v):
        v.memset(xqTv[:, :, 0:1], 0.0).then_inc(sem("p1z"), 1); inc("p1z")
        v.memset(xqTv[:, :, 2049:2050], 0.0).then_inc(sem("p1z"), 1); inc("p1z")

        def evac(i):
            for g in range(2):
                v.wait_ge(sem("p1tpm"), 2 * i + g + 1)
                if g == 0:
                    dst = xnTv[:, :, i * 128:(i + 1) * 128]
                else:
                    dst = xqTv[:, :, i * 128 + 1:(i + 1) * 128 + 1]
                v.tensor_copy(dst, psB.ap().rearrange("p (c t) -> p c t", c=8)
                              ).then_inc(sem("p1ev"), 1)
                inc("p1ev")

        for i in range(NT):
            b = i % 2
            c = lambda j: stt[:, b * 16 + j:b * 16 + j + 1]
            xt = x_t[:, b * 1024:(b + 1) * 1024]
            xnt = xn_t[:, b * 1024:(b + 1) * 1024]
            v.wait_ge(sem(f"xl{b}"), 16 * (i // 2 + 1))
            v.reduce_sum(c(1), xt, axis=AX).then_inc(sem("p1v"), 1); inc("p1v")
            v.wait_ge(sem("p1v"), val("p1v"))
            v.tensor_scalar(c(1), c(1), -1.0 / C, None, OP.mult).then_inc(sem("p1v"), 1); inc("p1v")
            v.wait_ge(sem("p1v"), val("p1v"))
            v.tensor_tensor(c(4), c(1), c(1), OP.mult).then_inc(sem("p1v"), 1); inc("p1v")
            v.wait_ge(sem("p1sq"), i + 1)
            v.wait_ge(sem("p1v"), val("p1v"))
            v.scalar_tensor_tensor(c(2), c(0), 1.0 / C, c(4),
                                   OP.mult, OP.subtract).then_inc(sem("p1v"), 1); inc("p1v")
            v.wait_ge(sem("p1v"), val("p1v"))
            v.tensor_scalar(c(2), c(2), EPS, None, OP.add).then_inc(sem("p1v"), 1); inc("p1v")
            v.wait_ge(sem("p1sqrt"), i + 1)
            v.reciprocal(c(5), c(3)).then_inc(sem("p1v"), 1); inc("p1v")
            v.wait_ge(sem("p1v"), val("p1v"))
            v.tensor_tensor(c(6), c(5), c(5), OP.mult).then_inc(sem("p1v"), 1); inc("p1v")
            v.wait_ge(sem("p1v"), val("p1v"))
            v.tensor_tensor(c(6), c(6), c(2), OP.mult).then_inc(sem("p1v"), 1); inc("p1v")
            v.wait_ge(sem("p1v"), val("p1v"))
            v.tensor_scalar(c(6), c(6), -0.5, 1.5, OP.mult, OP.add).then_inc(sem("p1v"), 1); inc("p1v")
            v.wait_ge(sem("p1v"), val("p1v"))
            v.tensor_tensor(c(7), c(6), c(5), OP.mult).then_inc(sem("p1v"), 1); inc("p1v")
            v.wait_ge(sem("p1v"), val("p1v"))
            if i >= 2:
                v.wait_ge(sem("p1xq"), i - 1)
            v.tensor_scalar(xnt, xt, c(1), c(7), OP.add, OP.mult).then_inc(
                sem(f"xf{b}"), 1)
            inc(f"xf{b}")
            v.wait_ge(sem(f"xf{b}"), val(f"xf{b}"))
            v.tensor_scalar(mask[:, i:i + 1], xnt[:, 0:1], 0.0, NMASK,
                            OP.is_equal, OP.mult).then_inc(sem("p1v"), 1); inc("p1v")
            if i >= 2:
                v.wait_ge(sem("p1ev"), 2 * (i - 1))
            v.tensor_copy(xnb_t[:, b * 1024:(b + 1) * 1024], xnt).then_inc(sem("p1xb"), 1); inc("p1xb")
            v.wait_ge(sem("p1v"), val("p1v"))
            v.reduce_max(c(8), xnt, axis=AX,
                         apply_absolute_value=True).then_inc(sem("p1v"), 1); inc("p1v")
            v.wait_ge(sem("p1v"), val("p1v"))
            v.tensor_scalar(c(9), c(8), 1e-5, None, OP.max).then_inc(sem("p1v"), 1); inc("p1v")
            v.wait_ge(sem("p1v"), val("p1v"))
            v.tensor_scalar(invsx[:, i:i + 1], c(9), 1.0 / 127.0, None,
                            OP.mult).then_inc(sem("p1v"), 1); inc("p1v")
            v.reciprocal(c(10), c(9)).then_inc(sem("p1v"), 1); inc("p1v")
            v.wait_ge(sem("p1v"), val("p1v"))
            v.tensor_scalar(c(11), c(10), 127.0, None, OP.mult).then_inc(sem("p1v"), 1); inc("p1v")
            v.wait_ge(sem("p1v"), val("p1v"))
            v.wait_ge(sem("p0v"), MEFF)
            v.tensor_scalar(xnt, xnt, c(11), meff[:, 0:1],
                            OP.mult, OP.add).then_inc(sem("p1q1"), 1); inc("p1q1")
            if i >= 1:
                evac(i - 1)
        evac(NT - 1)
        v.wait_ge(sem("p1v"), val("p1v"))
        v.tensor_scalar(bias_exp[:], mask[:], -B0, None, OP.add).then_inc(sem("p1bias"), 1); inc("p1bias")
    on("vector", p1_vector)
    P1EV = val("p1ev")
    P1BIAS = val("p1bias")
    for _g in (g_identb, g_stt, g_xqb, g_xnb, g_xn_t, g_x_t):
        _g.__exit__(None, None, None)
    g_psB.__exit__(None, None, None)
    psD = E(nc.psum_tensor("psD", [128, 512], F32))



    WABS = W1CH + W2CH

    def wmean_vec(v):
        v.wait_ge(sem("wabs"), WABS)
        v.reduce_sum(tiny[:, 2:3], wsum1[:], axis=AX).then_inc(sem("wmv"), 1); inc("wmv")
        v.reduce_sum(tiny[:, 3:4], wsum2[:], axis=AX).then_inc(sem("wmv"), 1); inc("wmv")
    on("vector", wmean_vec)
    WMV1 = val("wmv")

    def wmean_tensor(t):
        t.wait_ge(sem("p0v"), MEFF)
        t.wait_ge(sem("wmv"), WMV1)
        t.matmul(psC[0:1, 0:1], tiny[:, 2:3], onesc[:], start=True, stop=True)
        mm = t.matmul(psC[0:1, 1:2], tiny[:, 3:4], onesc[:], start=True, stop=True)
        mm.then_inc(sem("wmt"), 1); inc("wmt")
    on("tensor", wmean_tensor)
    WMT1 = val("wmt")

    def wmean_vec2(v):
        v.wait_ge(sem("wmt"), WMT1)
        v.tensor_scalar(tiny[0:1, 4:6], psC[0:1, 0:2], 1.0 / (C * DF), 1e-5,
                        OP.mult, OP.max).then_inc(sem("wmv"), 1); inc("wmv")
        v.wait_ge(sem("wmv"), val("wmv"))
        v.reciprocal(tiny[0:1, 6:8], tiny[0:1, 4:6]).then_inc(sem("wmv"), 1); inc("wmv")
    on("vector", wmean_vec2)
    WMV2 = val("wmv")

    def wmean_tensor2(t):
        t.wait_ge(sem("wmv"), WMV2)
        t.matmul(psC[:, 8:9], onesr[0:1, 0:128], tiny[0:1, 6:7], start=True, stop=True)
        t.matmul(psC[:, 9:10], onesr[0:1, 0:128], tiny[0:1, 7:8], start=True, stop=True)
        mm = t.matmul(psC[:, 10:11], onesr[0:1, 0:128], tiny[0:1, 5:6],
                      start=True, stop=True)
        mm.then_inc(sem("wmt"), 1); inc("wmt")
    on("tensor", wmean_tensor2)
    WMT2 = val("wmt")

    def wmean_vec3(v):
        v.wait_ge(sem("wmt"), WMT2)
        v.tensor_copy(sw1c[:], psC[:, 8:9]).then_inc(sem("wmv"), 1); inc("wmv")
        v.tensor_copy(sw2c[:], psC[:, 9:10]).then_inc(sem("wmv"), 1); inc("wmv")
        v.tensor_copy(mw2c[:], psC[:, 10:11]).then_inc(sem("wmv"), 1); inc("wmv")
        v.wait_ge(sem("wmv"), val("wmv"))
        v.reciprocal(tiny[:, 8:9], sw1c[:]).then_inc(sem("wmv"), 1); inc("wmv")
        v.wait_ge(sem("wmv"), val("wmv"))
        v.tensor_scalar(cws[:], cwb[:], tiny[:, 8:9], None,
                        OP.mult).then_inc(sem("wmv"), 1); inc("wmv")
    on("vector", wmean_vec3)
    WSC = val("wmv")

    g_wqb = nc.sbuf_tensor("wqb", [128, 2 * 1024], BF16)
    wqb = g_wqb.__enter__()

    NW = W1CH + W2CH

    def qdma2(ch, eng):
        src = w1_src(ch) if ch < W1CH else w2_src(ch - W1CH)
        eng.dma_start(out=wf[:, (ch % 4) * 1024:(ch % 4 + 1) * 1024],
                      in_=src).then_inc(sem(f"ql{ch % 4}"), 16)
        inc(f"ql{ch % 4}", 16)

    def wq_act1(s, ch):
        scl = sw1c if ch < W1CH else sw2c
        s.wait_ge(sem(f"ql{ch % 4}"), 16 * (ch // 4 + 1))
        s.activation(out=wf[:, (ch % 4) * 1024:(ch % 4 + 1) * 1024],
                     in_=wf[:, (ch % 4) * 1024:(ch % 4 + 1) * 1024],
                     func=AF.Identity,
                     bias=meff[:, 0:1], scale=scl[:, 0:1]).then_inc(sem("wqs"), 1)
        inc("wqs")

    def wq_act2(s, ch):
        s.wait_ge(sem("wqs"), 2 * ch + 1)
        s.activation(out=wf[:, (ch % 4) * 1024:(ch % 4 + 1) * 1024],
                     in_=wf[:, (ch % 4) * 1024:(ch % 4 + 1) * 1024],
                     func=AF.Identity,
                     bias=meffneg[:, 0:1]).then_inc(sem("wqs"), 1)
        inc("wqs")

    def wq_store(q):
        for ch in range(W1CH + W2CH):
            b = ch % 2
            q.wait_ge(sem("wqv"), ch + 1)
            if ch < W1CH:
                r, h = ch // 4, ch % 4
                dst = w1q_d[h * 8:(h + 1) * 8, :, r * 128:(r + 1) * 128].rearrange(
                    "j p m -> p j m")
                src = wqb[:, b * 1024:(b + 1) * 1024].rearrange("p (j m) -> p j m", j=8)
            else:
                dst = w2q_d[(ch - W1CH) * 128:(ch - W1CH + 1) * 128, :]
                src = wqb[:, b * 1024:(b + 1) * 1024]
            q.dma_start(out=dst, in_=src).then_inc(sem(f"wqst{b}"), 16)
            inc(f"wqst{b}", 16)
    on("sync", wq_store)
    WQST0 = val("wqst0")
    WQST1 = val("wqst1")

    w1q_sb = E(nc.sbuf_tensor("w1q_sb", [128, 2 * 1024], BF16, side="right"))
    w2q_sb = E(nc.sbuf_tensor("w2q_sb", [128, 4 * 512], BF16, side="right"))
    g_qT = nc.sbuf_tensor("qroll", [128, 2 * 2048], BF16, side="right"); qT = g_qT.__enter__()
    g_kT = nc.sbuf_tensor("kroll", [128, 2 * 2048], BF16, side="right"); kT = g_kT.__enter__()
    qTv = qT.ap().rearrange("p (s t) -> p s t", s=2)
    kTv = kT.ap().rearrange("p (s t) -> p s t", s=2)
    g_v_sb = nc.sbuf_tensor("vroll", [128, 2 * NT * 260], BF16, side="right"); v_sb = g_v_sb.__enter__()
    v_v = v_sb.ap().rearrange("p (s t h d) -> p s t h d", s=2, t=NT, h=4)
    g_probs = nc.sbuf_tensor("probs", [128, 3 * 1024], BF16, side="right"); probs = g_probs.__enter__()
    g_wblk = nc.sbuf_tensor("wblk", [128, 2 * 1024], BF16); wblk = g_wblk.__enter__()
    g_wvq = nc.sbuf_tensor("wvq", [128, 2 * 2048], BF16); wvq = g_wvq.__enter__()

    prods = ([("q", 0, qb) for qb in range(4)] + [("k", 0, qb) for qb in range(4)]
             + [("v", 0, ti) for ti in range(NT)])
    PROLOG = len(prods)
    seg_prod = {}
    for hp in range(NHP):
        lst = []
        if hp < NHP - 1:
            lst += [("q", hp + 1, qb) for qb in range(4)]
            lst += [("k", hp + 1, qb) for qb in range(4)]
        if hp < 6:
            lst += [("v", hp // 2 + 1, (hp % 2) * 8 + m) for m in range(8)]
        seg_prod[hp] = lst
        prods = prods + lst
    pidx = {item: i for i, item in enumerate(prods)}
    seg_sched = {}
    for hp in range(NHP):
        chunks = [(item, half) for item in seg_prod[hp] for half in (0, 1)]
        mp = {}
        for m, ch_ in enumerate(chunks):
            mp.setdefault(m * 64 // max(len(chunks), 1), []).append(ch_)
        seg_sched[hp] = mp
    NQK, NV = {}, {}
    _nq = _nv = 0
    for i, item in enumerate(prods):
        if item[0] == "v":
            _nv += 1
        else:
            _nq += 1
        NQK[i], NV[i] = _nq, _nv
    last_k = {hp: pidx[("k", hp, 3)] for hp in range(NHP)}

    def p23_gload(g):
        g.dma_start(out=bvb[:], in_=bv_d[:]).then_inc(sem("p2bv"), 16)
        inc("p2bv", 16)
        g.wait_ge(sem("p1xq"), NT)
        g.wait_ge(sem("p1tpm"), 2 * NT)
        g.wait_ge(sem("p1sq"), NT)
        g.dma_start(out=wblk[:, 0:1024], in_=wq_d[0]).then_inc(sem("lq"), 16); inc("lq", 16)
        g.dma_start(out=wblk[:, 1024:2048], in_=wk_d[0]).then_inc(sem("lk"), 16); inc("lk", 16)
        g.dma_start(out=wvq[:, 0:2048], in_=wv_d[0]).then_inc(sem("lv0"), 16); inc("lv0", 16)
        g.dma_start(out=wvq[:, 2048:4096], in_=wv_d[1]).then_inc(sem("lv1"), 16); inc("lv1", 16)
        for hp in range(1, NHP):
            g.wait_ge(sem("qkp"), 8 * (hp - 1) + 4)
            g.dma_start(out=wblk[:, 0:1024], in_=wq_d[hp]).then_inc(sem("lq"), 16); inc("lq", 16)
            g.wait_ge(sem("qkp"), 8 * hp)
            g.dma_start(out=wblk[:, 1024:2048], in_=wk_d[hp]).then_inc(sem("lk"), 16); inc("lk", 16)
            if hp == 2:
                g.wait_ge(sem("vmm"), 16)
                g.dma_start(out=wvq[:, 0:2048], in_=wv_d[2]).then_inc(sem("lv0"), 16); inc("lv0", 16)
            if hp == 4:
                g.wait_ge(sem("vmm"), 32)
                g.dma_start(out=wvq[:, 2048:4096], in_=wv_d[3]).then_inc(sem("lv1"), 16); inc("lv1", 16)
    on("gpsimd", p23_gload)

    def p23_tensor(t):
        t.wait_ge(sem("p0v"), MEFF)
        t.wait_ge(sem("p1ev"), P1EV)
        t.wait_ge(sem("p1bias"), P1BIAS)

        def prod_block(item, half=None):
            kind, a, b = item
            i = pidx[item]
            ccs = range(NCH) if half is None else range(half * 4, half * 4 + 4)
            if i >= 1 and half != 1:
                t.wait_ge(sem("pde"), i)
            if kind in ("q", "k"):
                if half != 1:
                    t.wait_ge(sem("lq" if kind == "q" else "lk"), 16 * (a + 1))
                off = 0 if kind == "q" else 1024
                for cc in ccs:
                    mm = t.matmul(psD[:, 0:512],
                                  wblk[:, off + cc * 128:off + (cc + 1) * 128],
                                  xnTv[:, cc, b * 512:(b + 1) * 512],
                                  start=(cc == 0), stop=(cc == NCH - 1))
                if half != 0:
                    mm.then_inc(sem("qkp"), 1)
                    inc("qkp")
            else:
                qv, ti = a, b
                if half != 1:
                    t.wait_ge(sem(f"lv{qv % 2}"), 16 * (qv // 2 + 1))
                for cc in ccs:
                    t.matmul(psD[:, 0:256],
                             xnTv[:, cc, ti * 128:(ti + 1) * 128],
                             wvq[:, (qv % 2) * 2048 + cc * 256:(qv % 2) * 2048 + (cc + 1) * 256],
                             start=(cc == 0), stop=False)
                if half != 0:
                    if qv == 0 and ti == 0:
                        t.wait_ge(sem("p2bv"), 16)
                    mm = t.matmul(psD[:, 0:256], onesb[0:1, 0:128],
                                  bvb[0:1, qv * 256:(qv + 1) * 256], start=False, stop=True)
                    mm.then_inc(sem("vmm"), 1)
                    inc("vmm")

        def emit_ctx(jj):
            itj, kcj = divmod(jj, NT)
            hpj = itj // NQB
            qvj = hpj // 2
            laj = 2 * (hpj % 2)
            if jj == 0:
                t.wait_ge(sem("vone"), 2)
            if kcj == 0:
                t.wait_ge(sem("p3ev2"), 3 * itj)
            t.wait_ge(sem("p3exp"), jj + 1)
            t.wait_ge(sem("pde"), pidx[("v", qvj, kcj)] + 1)
            pslot = jj % 3
            t.matmul(psC[0:65, 0:512],
                     v_v[:, qvj % 2, kcj, laj, 0:65],
                     probs[:, pslot * 1024:pslot * 1024 + 512],
                     start=(kcj == 0), stop=(kcj == NT - 1))
            mm = t.matmul(psC[0:65, 512:1024],
                          v_v[:, qvj % 2, kcj, laj + 1, 0:65],
                          probs[:, pslot * 1024 + 512:pslot * 1024 + 1024],
                          start=(kcj == 0), stop=(kcj == NT - 1))
            mm.then_inc(sem("p3pf"), 1)
            inc("p3pf")
            if kcj == NT - 1:
                t.wait_ge(sem("p3ev2"), 3 * itj + 3)
                if itj > 0:
                    t.wait_ge(sem("p3nv"), itj)
                t.matmul(psC[0:64, 1024:1536], onesr[0:1, 0:64], denrow[0:1, 0:512],
                         start=True, stop=True)
                mm = t.matmul(psC[64:128, 1024:1536], onesr[0:1, 0:64],
                              denrow[0:1, 512:1024], start=True, stop=True)
                mm.then_inc(sem("p3nb"), 1)
                inc("p3nb")

        for item in prods[:PROLOG]:
            prod_block(item)
        NJ3 = NHP * NQB * NT
        for j in range(NJ3):
            it, kc = divmod(j, NT)
            hp, qb = divmod(it, NQB)
            jl = j - hp * 64
            if jl == 0:
                t.wait_ge(sem("pde"), last_k[hp] + 1)
            pp = kc % 2
            sA = psA[:, pp * 1024:pp * 1024 + 512]
            sB = psA[:, pp * 1024 + 512:pp * 1024 + 1024]
            if j >= 2:
                t.wait_ge(sem("p3exp"), j - 1)
            t.matmul(sA, kTv[0:64, hp % 2, kc * 128:(kc + 1) * 128],
                     qTv[0:64, hp % 2, qb * 512:(qb + 1) * 512],
                     start=True, stop=True, tile_position=(0, 0))
            mm = t.matmul(sB, kTv[64:128, hp % 2, kc * 128:(kc + 1) * 128],
                          qTv[64:128, hp % 2, qb * 512:(qb + 1) * 512],
                          start=True, stop=True, tile_position=(64, 0))
            mm.then_inc(sem("p3s"), 1)
            inc("p3s")
            if j >= 1:
                emit_ctx(j - 1)
            for item, half in seg_sched[hp].get(jl, []):
                prod_block(item, half)
        emit_ctx(NJ3 - 1)
    on("tensor", p23_tensor)
    P3PF = val("p3pf")

    def p23_scalar(s):
        s.wait_ge(sem("wmv"), WSC)
        s.wait_ge(sem("p1bias"), P1BIAS)
        for ch in range(4):
            qdma2(ch, s)
        NJ3 = NHP * NQB * NT
        for j in range(NJ3):
            if j % 8 == 4:
                chl = j // 8 + 3
                if 4 <= chl < NW:
                    s.wait_ge(sem("wqv"), chl - 3)
                    qdma2(chl, s)
            pp = j % 2
            pslot = j % 3
            kcn = j % NT
            s.wait_ge(sem("p3s"), j + 1)
            if j >= 3:
                s.wait_ge(sem("p3pf"), j - 2)
            s.activation(out=probs[:, pslot * 1024:(pslot + 1) * 1024],
                         in_=psA[:, pp * 1024:(pp + 1) * 1024],
                         func=AF.Exp, scale=0.125,
                         bias=bias_exp[:, kcn:kcn + 1]).then_inc(sem("p3exp"), 1)
            inc("p3exp")
            if j % 8 == 2 and j // 8 < NW:
                wq_act1(s, j // 8)
            if j % 8 == 5 and j // 8 < NW:
                wq_act2(s, j // 8)
    on("scalar", p23_scalar)

    def p23_vec(v):
        v.memset(v_v[:, 0, :, :, 64:65], 1.0).then_inc(sem("vone"), 1); inc("vone")
        v.memset(v_v[:, 1, :, :, 64:65], 1.0).then_inc(sem("vone"), 1); inc("vone")

        def wclip(ch):
            b4 = ch % 4
            b = ch % 2
            v.wait_ge(sem("wqs"), 2 * ch + 2)
            if ch >= 2:
                v.wait_ge(sem(f"wqst{ch % 2}"), 16 * (ch // 2))
            v.tensor_scalar(wqb[:, b * 1024:(b + 1) * 1024],
                            wf[:, b4 * 1024:(b4 + 1) * 1024],
                            -1.0, 1.0, OP.max, OP.min).then_inc(sem("wqv"), 1); inc("wqv")

        def evac_prod(item):
            kind, a, b = item
            i = pidx[item]
            if kind == "v":
                v.wait_ge(sem("vmm"), NV[i])
                v.tensor_copy(v_v[:, a % 2, b, 0:4, 0:64],
                              psD[:, 0:256].rearrange("p (h d) -> p h d", h=4)
                              ).then_inc(sem("pde"), 1)
            else:
                v.wait_ge(sem("qkp"), NQK[i])
                dstv, bc = (qTv, bqc) if kind == "q" else (kTv, bkc)
                v.tensor_scalar(dstv[:, a % 2, b * 512:(b + 1) * 512],
                                psD[:, 0:512],
                                bc[:, a:a + 1], None, OP.add).then_inc(sem("pde"), 1)
            inc("pde")

        def ctx_norm(it):
            hp, qb = divmod(it, NQB)
            v.wait_ge(sem("p3nb"), it + 1)
            v.reciprocal(rtile[:], psC[:, 1024:1536]).then_inc(sem("p3nv"), 1); inc("p3nv")
            v.wait_ge(sem("p3nv"), val("p3nv"))
            v.tensor_tensor(ctxv[:, hp, qb * 512:(qb + 1) * 512],
                            ctxv[:, hp, qb * 512:(qb + 1) * 512],
                            rtile[:], OP.mult).then_inc(sem("p3tt"), 1); inc("p3tt")

        for item in prods[:PROLOG]:
            evac_prod(item)
        NJ3 = NHP * NQB * NT
        for j in range(NJ3):
            it, kc = divmod(j, NT)
            hp, qb = divmod(it, NQB)
            jl = j - hp * 64
            for item, half in seg_sched[hp].get(jl, []):
                if half == 1:
                    evac_prod(item)
            if j % 8 == 1 and 1 <= j // 8 <= NW:
                wclip(j // 8 - 1)
            if kc == 4 and it >= 1:
                ctx_norm(it - 1)
            if kc == NT - 1:
                v.wait_ge(sem("p3pf"), (it + 1) * NT)
                v.tensor_copy(ctxv[0:64, hp, qb * 512:(qb + 1) * 512],
                              psC[0:64, 0:512]).then_inc(sem("p3ev2"), 1); inc("p3ev2")
                v.tensor_copy(ctxv[64:128, hp, qb * 512:(qb + 1) * 512],
                              psC[0:64, 512:1024]).then_inc(sem("p3ev2"), 1); inc("p3ev2")
                v.tensor_copy(denrow[0:1, :],
                              psC[64:65, 0:1024]).then_inc(sem("p3ev2"), 1); inc("p3ev2")
        ctx_norm(NHP * NQB - 1)
        wclip(NW - 1)
    on("vector", p23_vec)
    P3TT = val("p3tt")

    for _g in (g_probs, g_v_sb, g_kT, g_qT):
        _g.__exit__(None, None, None)
    for _g in (g_wvq, g_wblk, g_wqb, g_wf, g_xnT):
        _g.__exit__(None, None, None)
    g_wo = nc.sbuf_tensor("wo_bf", [128, 8 * 1024], BF16); wo_bf = g_wo.__enter__()
    g_yat = nc.sbuf_tensor("y_attn", [128, NT * 1024], BF16, side="right"); y_attn = g_yat.__enter__()
    y_av = y_attn.ap().rearrange("p (t m) -> p t m", t=NT)

    def p35_gload(g):
        g.wait_ge(sem("qkp"), 8 * NHP)
        g.wait_ge(sem("vmm"), 4 * NT)
        g.wait_ge(sem("wqv"), NW)
        g.wait_ge(sem("wqst0"), WQST0)
        g.wait_ge(sem("wqst1"), WQST1)
        for cc in range(NCH):
            g.dma_start(out=wo_bf[:, cc * 1024:(cc + 1) * 1024],
                        in_=wo_d[cc * 128:(cc + 1) * 128, :]).then_inc(sem("p3wo"), 16)
            inc("p3wo", 16)
    on("gpsimd", p35_gload)
    P3WO = val("p3wo")

    def p35_tensor(t):
        t.wait_ge(sem("p3tt"), P3TT)
        t.wait_ge(sem("p3wo"), P3WO)
        for ti in range(NT):
            for co in range(2):
                n = ti * 2 + co
                slot = n % 3
                ps = psA[:, slot * 512:(slot + 1) * 512]
                if n >= 3:
                    t.wait_ge(sem("p35oev"), n - 2)
                for cc in range(NCH):
                    t.matmul(ps, ctxv[:, cc, ti * 128:(ti + 1) * 128],
                             wo_bf[:, cc * 1024 + co * 512:cc * 1024 + (co + 1) * 512],
                             start=(cc == 0), stop=False)
                mm = t.matmul(ps, onesr[0:1, 0:128], bor[0:1, co * 512:(co + 1) * 512],
                              start=False, stop=True)
                mm.then_inc(sem("p35o"), 1)
                inc("p35o")
    on("tensor", p35_tensor)

    def p35_vec(v):
        for ti in range(NT):
            for co in range(2):
                n = ti * 2 + co
                slot = n % 3
                v.wait_ge(sem("p35o"), n + 1)
                v.tensor_copy(y_av[:, ti, co * 512:(co + 1) * 512],
                              psA[:, slot * 512:(slot + 1) * 512]).then_inc(sem("p35oev"), 1); inc("p35oev")
    on("vector", p35_vec)
    P35OEV = val("p35oev")

    for _g in (g_wo, g_rr, g_dr, g_ctx, g_bvb, g_onesb, g_bor):
        _g.__exit__(None, None, None)
    h_act = E(nc.sbuf_tensor("h_act", [128, NJC * 512], F32))
    h_v = h_act.ap().rearrange("p (j t) -> p j t", j=NJC)
    hq = E(nc.sbuf_tensor("hq", [128, NJC * 512], BF16))
    hq_v = hq.ap().rearrange("p (j t) -> p j t", j=NJC)
    u_sb = E(nc.sbuf_tensor("u_sb", [128, 3 * 512], F32))
    qscr = E(nc.sbuf_tensor("qscr", [128, 2 * 512], F32))
    qscr2 = E(nc.sbuf_tensor("qscr2", [128, 512], F32))
    shb_sb = E(nc.sbuf_tensor("shb_sb", [128, 512], F32))
    amax1 = E(nc.sbuf_tensor("amax1", [128, 512], F32))
    amax2 = E(nc.sbuf_tensor("amax2", [128, 512], F32))
    xr_t = E(nc.sbuf_tensor("xr_t", [128, 4 * 512], F32))
    out_t = E(nc.sbuf_tensor("out_t", [128, 2 * 1024], F32))

    def w1q_src(jc):
        return w1q_d[jc]

    P35OT = val("p35o")

    def p4_scalar(s):
        s.wait_ge(sem("wqst0"), WQST0)
        s.wait_ge(sem("wqst1"), WQST1)

        def silu_one(tb, jc):
            r = tb * NJC + jc
            ub = r % 3
            s.wait_ge(sem("p4cv"), 3 * r + 3)
            if tb >= 1:
                s.wait_ge(sem("p4q"), (tb - 1) * NJC + jc + 1)
            s.activation(out=h_v[:, jc, 0:512],
                         in_=u_sb[:, ub * 512:(ub + 1) * 512],
                         func=AF.Silu, bias=cbc[:, jc:jc + 1]).then_inc(sem("p4si"), 1)
            inc("p4si")

        def round_one(tb, jc):
            r = tb * NJC + jc
            qs = r % 2
            s.wait_ge(sem("p4qa"), r + 1)
            if r >= 1:
                s.wait_ge(sem("p4q"), r)
            s.activation(out=qscr2[:],
                         in_=qscr[:, qs * 512:(qs + 1) * 512],
                         func=AF.Identity, bias=meff[:, 0:1]).then_inc(sem("p4qb"), 1)
            inc("p4qb")
            s.wait_ge(sem("p4qb"), val("p4qb"))
            s.activation(out=hq_v[:, jc, 0:512],
                         in_=qscr2[:],
                         func=AF.Identity, bias=meffneg[:, 0:1]).then_inc(sem("p4q"), 1)
            inc("p4q")

        def silu_block(tb):
            for jc in range(NJC):
                if jc >= 2:
                    silu_one(tb, jc - 2)
                if tb >= 1:
                    round_one(tb - 1, jc)
            silu_one(tb, NJC - 2)
            silu_one(tb, NJC - 1)

        def w2q_block(tb):
            for co in range(2):
                for jc in range(NJC):
                    rr = (tb * 2 + co) * NJC + jc
                    b2 = rr % 4
                    if b2 >= 2:
                        continue
                    if rr >= 4:
                        s.wait_ge(sem("p4m2c"), rr - 3)
                    s.dma_start(out=w2q_sb[:, b2 * 512:(b2 + 1) * 512],
                                in_=w2q_d[jc * 128:(jc + 1) * 128,
                                          co * 512:(co + 1) * 512]).then_inc(
                                    sem(f"p4w2l{b2}"), 16)
                    inc(f"p4w2l{b2}", 16)

        silu_block(0)
        for tb in range(1, NTB):
            silu_block(tb)
            w2q_block(tb - 1)
        for jc in range(NJC):
            round_one(NTB - 1, jc)
            for co in range(2):
                idx = 2 * jc + co
                b2 = idx % 4
                if b2 >= 2:
                    continue
                s.wait_ge(sem("p4m2c"), 192 + idx - 3)
                s.dma_start(out=w2q_sb[:, b2 * 512:(b2 + 1) * 512],
                            in_=w2q_d[jc * 128:(jc + 1) * 128,
                                      co * 512:(co + 1) * 512]).then_inc(
                                sem(f"p4w2l{b2}"), 16)
                inc(f"p4w2l{b2}", 16)
    on("scalar", p4_scalar)

    def p4_tensor(t):
        t.wait_ge(sem("p1z"), 2)
        t.wait_ge(sem("p35oev"), P35OEV)

        def mm1_block(tb, jc0, jc1):
            for jc in range(jc0, jc1):
                r = tb * NJC + jc
                b = r % 2
                pslot = jc % 2
                ps = psA[:, pslot * 1024:pslot * 1024 + 514]
                t.wait_ge(sem(f"p4w1l{b}"), 16 * (r // 2 + 1))
                if r >= 2:
                    t.wait_ge(sem("p4cv"), 3 * (r - 1))
                for cc in range(NCH):
                    t.matmul(ps[:, 0:512],
                             w1q_sb[:, b * 1024 + cc * 128:b * 1024 + (cc + 1) * 128],
                             xqTv[:, cc, tb * 512:tb * 512 + 512],
                             start=(cc == 0), stop=(cc == NCH - 1))
                for cc in range(NCH):
                    mm = t.matmul(ps[:, 512:514],
                                  w1q_sb[:, b * 1024 + cc * 128:b * 1024 + (cc + 1) * 128],
                                  xqTv[:, cc, tb * 512 + 512:tb * 512 + 514],
                                  start=(cc == 0), stop=(cc == NCH - 1))
                mm.then_inc(sem("p4m1"), 1)
                inc("p4m1")

        def chain_s1(tb):
            t.wait_ge(sem("p4sh"), 8 * tb + 1)
            for j in range(4):
                ao = AMX_OFF[j]
                mm = t.transpose(psA[:, ao:ao + 128],
                                 amax1[:, j * 128:(j + 1) * 128], identf[:])
            mm.then_inc(sem("p4sh"), 1)
            inc("p4sh")

        def chain_s2(tb):
            t.wait_ge(sem("p4sh"), 8 * tb + 4)
            for j in range(4):
                ao = AMX_OFF[j]
                mm = t.transpose(psA[0:1, ao:ao + 128],
                                 sh_cols[:, tb * 4 + j:tb * 4 + j + 1], identf[:])
            mm.then_inc(sem("p4sh"), 1)
            inc("p4sh")

        def chain_s3(tb):
            t.wait_ge(sem("p4sh"), 8 * tb + 6)
            for j in range(4):
                ao = AMX_OFF[j]
                mm = t.matmul(psA[:, ao:ao + 128],
                              onesr[0:1, 0:128], trow[0:1, j * 128:(j + 1) * 128],
                              start=True, stop=True)
            mm.then_inc(sem("p4sh"), 1)
            inc("p4sh")

        def mm2_block(tb):
            for co in range(2):
                for jc in range(NJC):
                    rr = (tb * 2 + co) * NJC + jc
                    b2 = rr % 4
                    t.wait_ge(sem(f"p4w2l{b2}"), 16 * (rr // 4 + 1))
                    t.wait_ge(sem("p4q"), tb * NJC + jc + 1)
                    if jc == 0:
                        t.wait_ge(sem("p4fin"), (tb * 2 + co) * 4)
                    for tt in range(4):
                        psm2 = psD[:, 0:512] if tt == 3 else psC[:, tt * 512:(tt + 1) * 512]
                        mm = t.matmul(psm2,
                                      hq_v[:, jc, tt * 128:(tt + 1) * 128],
                                      w2q_sb[:, b2 * 512:(b2 + 1) * 512],
                                      start=(jc == 0), stop=(jc == NJC - 1))
                    mm.then_inc(sem("p4m2c"), 1)
                    inc("p4m2c")

        def mm2_block_last(tb):
            t.wait_ge(sem("p4sh"), 8 * tb + 8)
            t.wait_ge(sem("p4cv"), 3 * (tb + 1) * NJC)
            t.wait_ge(sem("p4fin"), tb * 2 * 4)
            for jc in range(NJC):
                t.wait_ge(sem("p4q"), tb * NJC + jc + 1)
                for co in range(2):
                    idx = 2 * jc + co
                    b2 = idx % 4
                    t.wait_ge(sem(f"p4w2l{b2}"), 16 * (48 + idx // 4 + 1))
                    for tt in range(4):
                        if co == 0:
                            psm2 = psD[:, 0:512] if tt == 3 else psC[:, tt * 512:(tt + 1) * 512]
                        else:
                            psm2 = psA[:, tt * 512:(tt + 1) * 512]
                        mm = t.matmul(psm2,
                                      hq_v[:, jc, tt * 128:(tt + 1) * 128],
                                      w2q_sb[:, b2 * 512:(b2 + 1) * 512],
                                      start=(jc == 0), stop=(jc == NJC - 1))
                    mm.then_inc(sem("p4m2c"), 1)
                    inc("p4m2c")

        mm1_block(0, 0, NJC)
        for tb in range(1, NTB):
            mm1_block(tb, 0, 2)
            chain_s1(tb - 1)
            chain_s2(tb - 1)
            chain_s3(tb - 1)
            mm1_block(tb, 2, NJC)
            mm2_block(tb - 1)
        chain_s1(NTB - 1)
        chain_s2(NTB - 1)
        chain_s3(NTB - 1)
        mm2_block_last(NTB - 1)
    on("tensor", p4_tensor)

    def p4_vec(v):
        v.wait_ge(sem("wmv"), WSC)

        def chain_v_block(tb):
            v.wait_ge(sem("p4sh"), 8 * tb + 2)
            for j in range(4):
                ti = tb * 4 + j
                ao = AMX_OFF[j]
                v.reduce_max(tiny[:, 9:10], psA[:, ao:ao + 128],
                             axis=AX).then_inc(sem("p4v"), 1); inc("p4v")
                v.wait_ge(sem("p4v"), val("p4v"))
                v.tensor_scalar(tiny[:, 10:11], tiny[:, 9:10], 1e-5, None,
                                OP.max).then_inc(sem("p4v"), 1); inc("p4v")
                v.wait_ge(sem("p4v"), val("p4v"))
                v.tensor_scalar(tiny[:, 11:12], tiny[:, 10:11], 1.0 / 127.0, None,
                                OP.mult).then_inc(sem("p4v"), 1); inc("p4v")
                v.wait_ge(sem("p4v"), val("p4v"))
                v.tensor_tensor(shc[:, ti:ti + 1], tiny[:, 11:12], mw2c[:],
                                OP.mult).then_inc(sem("p4v"), 1); inc("p4v")
                v.reciprocal(tiny[:, 12:13], tiny[:, 10:11]).then_inc(sem("p4v"), 1); inc("p4v")
                v.wait_ge(sem("p4v"), val("p4v"))
                v.tensor_scalar(sh_cols[:, ti:ti + 1], tiny[:, 12:13], 127.0, None,
                                OP.mult).then_inc(sem("p4v"), 1); inc("p4v")
            v.wait_ge(sem("p4v"), val("p4v"))
            v.engine_nop().then_inc(sem("p4sh"), 1)
            inc("p4sh")
            v.engine_nop().then_inc(sem("p4sh"), 1)
            inc("p4sh")
            v.wait_ge(sem("p4sh"), 8 * tb + 5)
            v.tensor_copy(trow[0:1, 0:256], psA[0:1, 640:896])
            v.tensor_copy(trow[0:1, 256:512], psA[0:1, 1664:1920]).then_inc(sem("p4sh"), 1)
            inc("p4sh")
            v.wait_ge(sem("p4sh"), 8 * tb + 7)
            v.tensor_copy(shb_sb[:, 0:256], psA[:, 640:896])
            v.tensor_copy(shb_sb[:, 256:512], psA[:, 1664:1920]).then_inc(sem("p4sh"), 1)
            inc("p4sh")

        def finals_co(tb, co):
            if True:
                k_ = tb * 2 + co
                if tb == NTB - 1:
                    v.wait_ge(sem("p4m2c"), 192 + 63 + co)
                else:
                    v.wait_ge(sem("p4m2c"), (k_ + 1) * NJC)
                for tt in range(4):
                    ti = tb * 4 + tt
                    osl = out_t[:, (tt % 2) * 1024 + co * 512:(tt % 2) * 1024 + (co + 1) * 512]
                    v.wait_ge(sem(f"p4xl{tt}"), 16 * (k_ + 1))
                    v.wait_ge(sem("p35oev"), P35OEV)
                    s_cls = (tt % 2) * 2 + co
                    m_ = k_ // 2
                    if tt >= 2:
                        v.wait_ge(sem(f"p4st{s_cls}"), 16 * (2 * m_ + 1))
                    elif m_ >= 1:
                        v.wait_ge(sem(f"p4st{s_cls}"), 16 * (2 * m_))
                    v.tensor_tensor(
                        osl,
                        xr_t[:, tt * 512:(tt + 1) * 512],
                        y_av[:, ti, co * 512:(co + 1) * 512],
                        OP.add).then_inc(sem("p4f1"), 1); inc("p4f1")
                    v.wait_ge(sem("p4f1"), val("p4f1"))
                    if tb == NTB - 1 and co == 1:
                        psff = psA[:, tt * 512:(tt + 1) * 512]
                    elif tt == 3:
                        psff = psD[:, 0:512]
                    else:
                        psff = psC[:, tt * 512:(tt + 1) * 512]
                    v.scalar_tensor_tensor(
                        osl,
                        psff,
                        shc[:, ti:ti + 1],
                        osl,
                        OP.mult, OP.add).then_inc(sem("p4fin"), 1); inc("p4fin")

        hT = h_act.ap().rearrange("p (j t) -> p t j", j=NJC)

        def amax_piece(tb, k):
            v.wait_ge(sem("p4si"), tb * NJC + 4 * k + 4)
            v.wait_ge(sem("p4fo"), val("p4fo"))
            if k == 0:
                if tb >= 1:
                    v.wait_ge(sem("p4sh"), 8 * (tb - 1) + 2)
                v.reduce_max(amax1[:], hT[:, :, 0:4], axis=AX,
                             apply_absolute_value=True).then_inc(sem("p4fo"), 1)
                inc("p4fo")
                return
            v.reduce_max(amax2[:], hT[:, :, 4 * k:4 * k + 4], axis=AX,
                         apply_absolute_value=True).then_inc(sem("p4fo"), 1)
            inc("p4fo")
            v.wait_ge(sem("p4fo"), val("p4fo"))
            mm = v.tensor_max(amax1[:], amax1[:], amax2[:])
            if k == 7:
                mm.then_inc(sem("p4sh"), 1)
                inc("p4sh")
            else:
                mm.then_inc(sem("p4fo"), 1)
                inc("p4fo")

        def conv_one(tb, jc):
            r = tb * NJC + jc
            pslot = jc % 2
            ub = r % 3
            ps = psA[:, pslot * 1024:pslot * 1024 + 514]
            v.wait_ge(sem("p4m1"), r + 1)
            if r >= 3:
                v.wait_ge(sem("p4si"), r - 2)
            v.tensor_scalar(u_sb[:, ub * 512:(ub + 1) * 512], ps[:, 0:512],
                            cws[:, jc * 3 + 0:jc * 3 + 1], None,
                            OP.mult).then_inc(sem("p4cv"), 1); inc("p4cv")
            v.wait_ge(sem("p4cv"), val("p4cv"))
            v.scalar_tensor_tensor(u_sb[:, ub * 512:(ub + 1) * 512],
                                   ps[:, 1:513], cws[:, jc * 3 + 1:jc * 3 + 2],
                                   u_sb[:, ub * 512:(ub + 1) * 512],
                                   OP.mult, OP.add).then_inc(sem("p4cv"), 1); inc("p4cv")
            v.wait_ge(sem("p4cv"), val("p4cv"))
            v.scalar_tensor_tensor(u_sb[:, ub * 512:(ub + 1) * 512],
                                   ps[:, 2:514], cws[:, jc * 3 + 2:jc * 3 + 3],
                                   u_sb[:, ub * 512:(ub + 1) * 512],
                                   OP.mult, OP.add).then_inc(sem("p4cv"), 1); inc("p4cv")

        def quant_one(tb, jc):
            r = tb * NJC + jc
            qs = r % 2
            v.wait_ge(sem("p4sh"), 8 * tb + 8)
            if r >= 2:
                v.wait_ge(sem("p4qb"), r - 1)
            v.tensor_tensor(qscr[:, qs * 512:(qs + 1) * 512], h_v[:, jc, 0:512],
                            shb_sb[:], OP.mult).then_inc(sem("p4qa"), 1); inc("p4qa")

        for jc in range(NJC):
            conv_one(0, jc)
            if jc >= 6 and (jc - 6) % 4 == 0:
                amax_piece(0, (jc - 6) // 4)
        amax_piece(0, 7)
        for tb in range(1, NTB):
            if tb >= 2:
                finals_co(tb - 2, 1)
            chain_v_block(tb - 1)
            for jc in range(NJC):
                quant_one(tb - 1, jc)
                conv_one(tb, jc)
                if jc >= 6 and (jc - 6) % 4 == 0:
                    amax_piece(tb, (jc - 6) // 4)
            amax_piece(tb, 7)
            finals_co(tb - 1, 0)
        finals_co(NTB - 2, 1)
        chain_v_block(NTB - 1)
        for jc in range(NJC):
            quant_one(NTB - 1, jc)
        finals_co(NTB - 1, 0)
        finals_co(NTB - 1, 1)
    on("vector", p4_vec)

    def p4_gsync(g):
        g.wait_ge(sem("wqst0"), WQST0)
        g.wait_ge(sem("wqst1"), WQST1)

        def w1q_block(tb):
            for jc in range(NJC):
                r = tb * NJC + jc
                b = r % 2
                if r >= 2:
                    g.wait_ge(sem("p4m1"), r - 1)
                g.dma_start(out=w1q_sb[:, b * 1024:(b + 1) * 1024],
                            in_=w1q_src(jc)).then_inc(sem(f"p4w1l{b}"), 16)
                inc(f"p4w1l{b}", 16)

        def x_block(k_):
            co = k_ % 2
            tb = k_ // 2
            if k_ == 0:
                g.wait_ge(sem("p35o"), P35OT)
            for tt in range(4):
                ti = tb * 4 + tt
                if k_ >= 1:
                    g.wait_ge(sem("p4f1"), (k_ - 1) * 4 + tt + 1)
                g.dma_start(out=xr_t[:, tt * 512:(tt + 1) * 512],
                            in_=x_d[ti * 128:(ti + 1) * 128,
                                    co * 512:(co + 1) * 512]).then_inc(sem(f"p4xl{tt}"), 16)
                inc(f"p4xl{tt}", 16)

        w1q_block(0)
        x_block(0)
        w1q_block(1)
        x_block(1)
        w1q_block(2)
        x_block(2)
        x_block(3)
        w1q_block(3)
        x_block(4)
        x_block(5)
        x_block(6)
        x_block(7)
    on("gpsimd", p4_gsync)

    def p4_sync(q):
        q.wait_ge(sem("wqst0"), WQST0)
        q.wait_ge(sem("wqst1"), WQST1)

        def w2q_sloads(order):
            for rr, jc, co in order:
                b2 = rr % 4
                if b2 < 2:
                    continue
                if rr >= 4:
                    q.wait_ge(sem("p4m2c"), rr - 3)
                q.dma_start(out=w2q_sb[:, b2 * 512:(b2 + 1) * 512],
                            in_=w2q_d[jc * 128:(jc + 1) * 128,
                                      co * 512:(co + 1) * 512]).then_inc(
                                sem(f"p4w2l{b2}"), 16)
                inc(f"p4w2l{b2}", 16)

        def stores_co(tb, co):
            for tt in range(4):
                ti = tb * 4 + tt
                n = (tb * 2 + co) * 4 + tt
                q.wait_ge(sem("p4fin"), n + 1)
                s_cls = (tt % 2) * 2 + co
                q.dma_start(out=out_d[ti * 128:(ti + 1) * 128,
                                      co * 512:(co + 1) * 512],
                            in_=out_t[:, (tt % 2) * 1024 + co * 512:(tt % 2) * 1024 + (co + 1) * 512]
                            ).then_inc(sem(f"p4st{s_cls}"), 16)
                inc(f"p4st{s_cls}", 16)

        for tb in range(NTB):
            if tb < NTB - 1:
                w2q_sloads([((tb * 2) * NJC + jc, jc, 0) for jc in range(NJC)])
                stores_co(tb, 0)
                w2q_sloads([((tb * 2 + 1) * NJC + jc, jc, 1) for jc in range(NJC)])
                stores_co(tb, 1)
            else:
                w2q_sloads([(tb * 2 * NJC + 2 * jc + co, jc, co)
                            for jc in range(NJC) for co in range(2)])
                stores_co(tb, 0)
                stores_co(tb, 1)
        for s_cls in range(4):
            q.wait_ge(sem(f"p4st{s_cls}"), val(f"p4st{s_cls}"))
    on("sync", p4_sync)

    g_yat.__exit__(None, None, None)
    st.close()
    return nc


def colswizzle_test(w, nb, bw):
    wt = np.asarray(w, dtype=np.float32).T
    return np.ascontiguousarray(
        wt.reshape(8, 128, nb, bw).transpose(2, 1, 0, 3).reshape(nb, 128, 8 * bw))


_NC_CACHE = [None]


def kernel(L0_attn_x, L0_attn_wq_w, L0_attn_wq_b, L0_attn_wk_w, L0_attn_wk_b,
           L0_attn_wv_w, L0_attn_wv_b, L0_attn_wo_w, L0_attn_wo_b,
           L0_ffn_w1, L0_ffn_conv_w, L0_ffn_conv_b, L0_ffn_w2):
    f32 = lambda a: np.ascontiguousarray(np.asarray(a, dtype=np.float32))
    x = f32(L0_attn_x)

    def colswizzle(w, nb, bw):
        wt = np.asarray(w, dtype=np.float32).T
        return np.ascontiguousarray(
            wt.reshape(8, 128, nb, bw).transpose(2, 1, 0, 3).reshape(nb, 128, 8 * bw))

    wq_t = colswizzle(L0_attn_wq_w, 8, 128)
    wk_t = colswizzle(L0_attn_wk_w, 8, 128)
    wv_t = colswizzle(L0_attn_wv_w, 4, 256)
    wo_t = f32(np.asarray(L0_attn_wo_w).T)
    w1_t = f32(np.asarray(L0_ffn_w1).T)
    w2_t = f32(np.asarray(L0_ffn_w2).T)
    cw = f32(np.asarray(L0_ffn_conv_w).reshape(32, 128, 3).transpose(1, 0, 2).reshape(128, 96))
    cb = f32(np.asarray(L0_ffn_conv_b).reshape(32, 128).T)
    bq = f32(np.asarray(L0_attn_wq_b).reshape(8, 128).T)
    bk = f32(np.asarray(L0_attn_wk_b).reshape(8, 128).T)
    bv = f32(np.asarray(L0_attn_wv_b).reshape(1, C))
    bo = f32(np.asarray(L0_attn_wo_b).reshape(1, C))
    identf = np.eye(128, dtype=np.float32)
    identb = np.eye(128, dtype=ml_dtypes.bfloat16)
    onesr = np.ones((1, 128), dtype=np.float32)
    if _NC_CACHE[0] is None:
        _NC_CACHE[0] = _build()
    nc = _NC_CACHE[0]

    shared = dict(wq_t=wq_t, wk_t=wk_t, wv_t=wv_t, wo_t=wo_t, w1_t=w1_t,
                  w2_t=w2_t, cw=cw, cb=cb, bq=bq, bk=bk, bv=bv, bo=bo,
                  identf=identf, identb=identb, onesr=onesr)
    in_maps = [dict(shared, x=np.ascontiguousarray(x[b])) for b in range(8)]
    res = run_bass_kernel_spmd(nc, in_maps, core_ids=list(range(8)))
    out = np.stack([np.asarray(res.results[b]["out"], dtype=np.float32)
                    for b in range(8)], axis=0)
    return out


# version 24
# speedup vs baseline: 1.0173x; 1.0173x over previous
import numpy as np
import ml_dtypes
from contextlib import ExitStack

import concourse.bass as bass
import concourse.mybir as mybir
from concourse.bass_utils import run_bass_kernel_spmd
from concourse.alu_op_type import AluOpType as OP

F32 = mybir.dt.float32
BF16 = mybir.dt.bfloat16
AF = mybir.ActivationFunctionType
AX = mybir.AxisListType.X

T, C, H, DH, DF = 2048, 1024, 16, 64, 4096
NT = T // 128
NCH = C // 128
NJC = DF // 128
NHP = H // 2
QB = 512
NQB = T // QB
TBS = 512
NTB = T // TBS
M32 = 12582912.0
M64 = 6755399441055744.0
EPS = 1e-5
B0 = 20.0
NMASK = -1.0e9
W1CH = 32
W2CH = 32
AMX_OFF = (640, 768, 1664, 1792)


def _build():
    nc = bass.Bass("TRN2", target_bir_lowering=False, debug=False)

    def din(name, shape, dt=F32):
        return nc.dram_tensor(name, list(shape), dt, kind="ExternalInput").ap()

    x_d = din("x", (T, C))
    wq_d = din("wq_t", (8, 128, 1024)); wk_d = din("wk_t", (8, 128, 1024))
    wv_d = din("wv_t", (4, 128, 2048)); wo_d = din("wo_t", (C, C))
    w1_d = din("w1_t", (C, DF)); w2_d = din("w2_t", (DF, C))
    cw_d = din("cw", (128, 96)); cb_d = din("cb", (128, 32))
    bq_d = din("bq", (128, 8)); bk_d = din("bk", (128, 8))
    bv_d = din("bv", (1, C)); bo_d = din("bo", (1, C))
    idf_d = din("identf", (128, 128))
    idb_d = din("identb", (128, 128), BF16)
    ones_d = din("onesr", (1, 128))

    out_d = nc.dram_tensor("out", [T, C], F32, kind="ExternalOutput").ap()
    w1q_d = nc.dram_tensor("w1q_stage", [NJC, 128, 1024], BF16, kind="Internal").ap()
    w2q_d = nc.dram_tensor("w2q_stage", [DF, C], BF16, kind="Internal").ap()

    st = ExitStack()
    E = st.enter_context

    semn, cnt = {}, {}
    def sem(name):
        if name not in semn:
            semn[name] = E(nc.semaphore(name))
            cnt[name] = 0
        return semn[name]
    def inc(name, by=1):
        sem(name); cnt[name] += by
        return cnt[name]
    def val(name):
        sem(name); return cnt[name]

    identf = E(nc.sbuf_tensor("identf_s", [128, 128], F32))
    onesr = E(nc.sbuf_tensor("onesr_s", [1, 128], F32))
    onesc = E(nc.sbuf_tensor("onesc", [128, 1], F32))
    bqc = E(nc.sbuf_tensor("bqc", [128, 8], F32))
    bkc = E(nc.sbuf_tensor("bkc", [128, 8], F32))
    cwb = E(nc.sbuf_tensor("cwb", [128, 96], F32))
    cws = E(nc.sbuf_tensor("cws", [128, 96], F32))
    cbc = E(nc.sbuf_tensor("cbc", [128, 32], F32))
    meff = E(nc.sbuf_tensor("meff", [128, 1], F32))
    meffneg = E(nc.sbuf_tensor("meffneg", [128, 1], F32))
    mask = E(nc.sbuf_tensor("mask", [128, 16], F32))
    bias_exp = E(nc.sbuf_tensor("bias_exp", [128, 16], F32))
    invsx = E(nc.sbuf_tensor("invsx", [128, 16], F32))
    shc = E(nc.sbuf_tensor("shc", [128, 16], F32))
    sh_cols = E(nc.sbuf_tensor("sh_cols", [128, 16], F32))
    wsum1 = E(nc.sbuf_tensor("wsum1", [128, W1CH], F32))
    wsum2 = E(nc.sbuf_tensor("wsum2", [128, W2CH], F32))
    sw1c = E(nc.sbuf_tensor("sw1c", [128, 1], F32))
    sw2c = E(nc.sbuf_tensor("sw2c", [128, 1], F32))
    mw2c = E(nc.sbuf_tensor("mw2c", [128, 1], F32))
    tiny = E(nc.sbuf_tensor("tiny", [128, 16], F32))
    trow = E(nc.sbuf_tensor("trow", [1, 512], F32))
    xqT = E(nc.sbuf_tensor("xqT", [128, NCH * 2050], BF16))
    xqTv = xqT.ap().rearrange("p (c t) -> p c t", c=NCH)
    g_bor = nc.sbuf_tensor("bor", [1, 1024], F32); bor = g_bor.__enter__()
    g_onesb = nc.sbuf_tensor("onesb", [1, 128], BF16); onesb = g_onesb.__enter__()
    g_bvb = nc.sbuf_tensor("bvb", [1, 1024], BF16); bvb = g_bvb.__enter__()
    g_ctx = nc.sbuf_tensor("ctx", [128, NCH * 2048], BF16); ctx = g_ctx.__enter__()
    ctxv = ctx.ap().rearrange("p (c t) -> p c t", c=NCH)
    g_dr = nc.sbuf_tensor("denrow", [1, 1024], F32); denrow = g_dr.__enter__()
    g_rr = nc.sbuf_tensor("rtile", [128, 512], F32); rtile = g_rr.__enter__()

    def w1_src(ch):
        r, h = ch // 4, ch % 4
        return w1_d[r * 128:(r + 1) * 128, h * 1024:(h + 1) * 1024]

    def w2_src(ch):
        return w2_d[ch * 128:(ch + 1) * 128, :]

    psA = E(nc.psum_tensor("psA", [128, 2048], F32))
    psC = E(nc.psum_tensor("psC", [128, 1536], F32))
    g_psB = nc.psum_tensor("psB", [128, 1024], BF16)
    psB = g_psB.__enter__()

    blk = E(nc.Block())
    def on(engname, fn):
        getattr(blk, engname)(fn)

    def p0_sync(q):
        for dst, src in [
            (identf[:], idf_d[:]), (onesr[:], ones_d[:]),
            (bqc[:], bq_d[:]), (bkc[:], bk_d[:]),
            (bor[:], bo_d[:]),
            (cwb[:], cw_d[:]), (cbc[:], cb_d[:]),
        ]:
            q.dma_start(out=dst, in_=src).then_inc(sem("cst"), 16)
            inc("cst", 16)
    on("sync", p0_sync)
    CST = val("cst")

    def p0_vec(v):
        v.wait_ge(sem("cst"), CST)
        v.memset(onesb[:], 1.0).then_inc(sem("p0v"), 1); inc("p0v")
        v.memset(onesc[:], 1.0).then_inc(sem("p0v"), 1); inc("p0v")
        v.wait_ge(sem("p0v"), val("p0v"))
        v.tensor_scalar(tiny[:, 0:1], onesc[:], 2.0 ** -30, 1.0,
                        OP.add, OP.subtract).then_inc(sem("p0v"), 1); inc("p0v")
        v.wait_ge(sem("p0v"), val("p0v"))
        v.tensor_scalar(tiny[:, 1:2], tiny[:, 0:1], 2.0 ** 30, None,
                        OP.mult).then_inc(sem("p0v"), 1); inc("p0v")
        v.wait_ge(sem("p0v"), val("p0v"))
        v.tensor_scalar(meff[:], tiny[:, 1:2], M64, M32,
                        OP.mult, OP.add).then_inc(sem("p0v"), 1); inc("p0v")
        v.wait_ge(sem("p0v"), val("p0v"))
        v.tensor_scalar(meffneg[:], meff[:], -1.0, None,
                        OP.mult).then_inc(sem("p0v"), 1); inc("p0v")
    on("vector", p0_vec)
    MEFF = val("p0v")

    g_xnT = nc.sbuf_tensor("xnT", [128, NCH * 2048], BF16)
    xnT = g_xnT.__enter__()
    xnTv = xnT.ap().rearrange("p (c t) -> p c t", c=NCH)
    g_wf = nc.sbuf_tensor("wf", [128, 4 * 1024], F32)
    wf = g_wf.__enter__()
    g_x_t = nc.sbuf_tensor("x_t", [128, 2 * 1024], F32); x_t = g_x_t.__enter__()
    g_xn_t = nc.sbuf_tensor("xn_t", [128, 2 * 1024], F32); xn_t = g_xn_t.__enter__()
    g_xnb = nc.sbuf_tensor("xnb_t", [128, 2 * 1024], BF16); xnb_t = g_xnb.__enter__()
    g_xqb = nc.sbuf_tensor("xqb_t", [128, 2 * 1024], BF16); xqb_t = g_xqb.__enter__()
    g_stt = nc.sbuf_tensor("stt", [128, 2 * 16], F32); stt = g_stt.__enter__()
    g_identb = nc.sbuf_tensor("identb_s", [128, 128], BF16)
    identb = g_identb.__enter__()

    def p1_sync(q):
        q.dma_start(out=identb[:], in_=idb_d[:]).then_inc(sem("cstb"), 16)
        inc("cstb", 16)
        for i in range(NT):
            b = i % 2
            if i >= 2:
                q.wait_ge(sem(f"xf{b}"), i // 2)
            q.dma_start(out=x_t[:, b * 1024:(b + 1) * 1024],
                        in_=x_d[i * 128:(i + 1) * 128, :]).then_inc(sem(f"xl{b}"), 16)
            inc(f"xl{b}", 16)
    on("sync", p1_sync)

    def p1_tensor(t):
        t.wait_ge(sem("cst"), CST)
        t.wait_ge(sem("cstb"), 16)
        for i in range(NT):
            b = i % 2
            for g, semname in enumerate(["p1xb", "p1xq"]):
                srcbuf = xnb_t if g == 0 else xqb_t
                t.wait_ge(sem(semname), i + 1)
                if 2 * i + g >= 1:
                    t.wait_ge(sem("p1ev"), 2 * i + g)
                for j in range(8):
                    mm = t.transpose(psB[:, j * 128:(j + 1) * 128],
                                     srcbuf[:, b * 1024 + j * 128:b * 1024 + (j + 1) * 128],
                                     identb[:])
                mm.then_inc(sem("p1tpm"), 1)
                inc("p1tpm")
    on("tensor", p1_tensor)

    def p1_scalar(s):
        NW = W1CH + W2CH

        def wdma(ch):
            if ch >= 2:
                s.wait_ge(sem("wabs"), ch - 1)
            src = w1_src(ch) if ch < W1CH else w2_src(ch - W1CH)
            s.dma_start(out=wf[:, (ch % 3) * 1024:(ch % 3 + 1) * 1024],
                        in_=src).then_inc(sem(f"wl{ch % 3}"), 16)
            inc(f"wl{ch % 3}", 16)

        def wabs(ch):
            acc = wsum1[:, ch:ch + 1] if ch < W1CH else wsum2[:, ch - W1CH:ch - W1CH + 1]
            s.wait_ge(sem(f"wl{ch % 3}"), 16 * (ch // 3 + 1))
            s.activation(out=wf[:, (ch % 3) * 1024:(ch % 3 + 1) * 1024],
                         in_=wf[:, (ch % 3) * 1024:(ch % 3 + 1) * 1024],
                         func=AF.Abs, accum_out=acc).then_inc(sem("wabs"), 1)
            inc("wabs")

        def quant_tail(i2):
            b2 = i2 % 2
            xnt2 = xn_t[:, b2 * 1024:(b2 + 1) * 1024]
            s.wait_ge(sem("p1q1"), i2 + 1)
            s.activation(out=xnt2, in_=xnt2, func=AF.Identity,
                         bias=meffneg[:, 0:1]).then_inc(sem("p1rt"), 1)
            inc("p1rt")
            s.wait_ge(sem("p1rt"), val("p1rt"))
            if i2 >= 2:
                s.wait_ge(sem("p1tpm"), 2 * i2 - 2)
            s.activation(out=xqb_t[:, b2 * 1024:(b2 + 1) * 1024], in_=xnt2,
                         func=AF.Identity,
                         scale=invsx[:, i2:i2 + 1]).then_inc(sem("p1xq"), 1)
            inc("p1xq")

        wdma(0); wdma(1)
        s.wait_ge(sem("p0v"), MEFF)
        for i in range(NT):
            b = i % 2
            s.wait_ge(sem(f"xl{b}"), 16 * (i // 2 + 1))
            if i >= 1:
                s.wait_ge(sem("p1sq"), i)
            s.activation(out=psA[:, 0:1024], in_=x_t[:, b * 1024:(b + 1) * 1024],
                         func=AF.Square,
                         accum_out=stt[:, b * 16 + 0:b * 16 + 1]).then_inc(sem("p1sq"), 1)
            inc("p1sq")
            for ch in (4 * i, 4 * i + 1):
                wabs(ch)
                if ch + 2 < NW:
                    wdma(ch + 2)
            if i >= 1:
                quant_tail(i - 1)
            s.wait_ge(sem("p1v"), 16 * i + 5)
            s.activation(out=stt[:, b * 16 + 3:b * 16 + 4],
                         in_=stt[:, b * 16 + 2:b * 16 + 3],
                         func=AF.Sqrt).then_inc(sem("p1sqrt"), 1)
            inc("p1sqrt")
            for ch in (4 * i + 2, 4 * i + 3):
                wabs(ch)
                if ch + 2 < NW:
                    wdma(ch + 2)
        quant_tail(NT - 1)
    on("scalar", p1_scalar)

    def p1_vector(v):
        v.memset(xqTv[:, :, 0:1], 0.0).then_inc(sem("p1z"), 1); inc("p1z")
        v.memset(xqTv[:, :, 2049:2050], 0.0).then_inc(sem("p1z"), 1); inc("p1z")

        def evac(i):
            for g in range(2):
                v.wait_ge(sem("p1tpm"), 2 * i + g + 1)
                if g == 0:
                    dst = xnTv[:, :, i * 128:(i + 1) * 128]
                else:
                    dst = xqTv[:, :, i * 128 + 1:(i + 1) * 128 + 1]
                v.tensor_copy(dst, psB.ap().rearrange("p (c t) -> p c t", c=8)
                              ).then_inc(sem("p1ev"), 1)
                inc("p1ev")

        for i in range(NT):
            b = i % 2
            c = lambda j: stt[:, b * 16 + j:b * 16 + j + 1]
            xt = x_t[:, b * 1024:(b + 1) * 1024]
            xnt = xn_t[:, b * 1024:(b + 1) * 1024]
            v.wait_ge(sem(f"xl{b}"), 16 * (i // 2 + 1))
            v.reduce_sum(c(1), xt, axis=AX).then_inc(sem("p1v"), 1); inc("p1v")
            v.wait_ge(sem("p1v"), val("p1v"))
            v.tensor_scalar(c(1), c(1), -1.0 / C, None, OP.mult).then_inc(sem("p1v"), 1); inc("p1v")
            v.wait_ge(sem("p1v"), val("p1v"))
            v.tensor_tensor(c(4), c(1), c(1), OP.mult).then_inc(sem("p1v"), 1); inc("p1v")
            v.wait_ge(sem("p1sq"), i + 1)
            v.wait_ge(sem("p1v"), val("p1v"))
            v.scalar_tensor_tensor(c(2), c(0), 1.0 / C, c(4),
                                   OP.mult, OP.subtract).then_inc(sem("p1v"), 1); inc("p1v")
            v.wait_ge(sem("p1v"), val("p1v"))
            v.tensor_scalar(c(2), c(2), EPS, None, OP.add).then_inc(sem("p1v"), 1); inc("p1v")
            v.wait_ge(sem("p1sqrt"), i + 1)
            v.reciprocal(c(5), c(3)).then_inc(sem("p1v"), 1); inc("p1v")
            v.wait_ge(sem("p1v"), val("p1v"))
            v.tensor_tensor(c(6), c(5), c(5), OP.mult).then_inc(sem("p1v"), 1); inc("p1v")
            v.wait_ge(sem("p1v"), val("p1v"))
            v.tensor_tensor(c(6), c(6), c(2), OP.mult).then_inc(sem("p1v"), 1); inc("p1v")
            v.wait_ge(sem("p1v"), val("p1v"))
            v.tensor_scalar(c(6), c(6), -0.5, 1.5, OP.mult, OP.add).then_inc(sem("p1v"), 1); inc("p1v")
            v.wait_ge(sem("p1v"), val("p1v"))
            v.tensor_tensor(c(7), c(6), c(5), OP.mult).then_inc(sem("p1v"), 1); inc("p1v")
            v.wait_ge(sem("p1v"), val("p1v"))
            if i >= 2:
                v.wait_ge(sem("p1xq"), i - 1)
            v.tensor_scalar(xnt, xt, c(1), c(7), OP.add, OP.mult).then_inc(
                sem(f"xf{b}"), 1)
            inc(f"xf{b}")
            v.wait_ge(sem(f"xf{b}"), val(f"xf{b}"))
            v.tensor_scalar(mask[:, i:i + 1], xnt[:, 0:1], 0.0, NMASK,
                            OP.is_equal, OP.mult).then_inc(sem("p1v"), 1); inc("p1v")
            if i >= 2:
                v.wait_ge(sem("p1ev"), 2 * (i - 1))
            v.tensor_copy(xnb_t[:, b * 1024:(b + 1) * 1024], xnt).then_inc(sem("p1xb"), 1); inc("p1xb")
            v.wait_ge(sem("p1v"), val("p1v"))
            v.reduce_max(c(8), xnt, axis=AX,
                         apply_absolute_value=True).then_inc(sem("p1v"), 1); inc("p1v")
            v.wait_ge(sem("p1v"), val("p1v"))
            v.tensor_scalar(c(9), c(8), 1e-5, None, OP.max).then_inc(sem("p1v"), 1); inc("p1v")
            v.wait_ge(sem("p1v"), val("p1v"))
            v.tensor_scalar(invsx[:, i:i + 1], c(9), 1.0 / 127.0, None,
                            OP.mult).then_inc(sem("p1v"), 1); inc("p1v")
            v.reciprocal(c(10), c(9)).then_inc(sem("p1v"), 1); inc("p1v")
            v.wait_ge(sem("p1v"), val("p1v"))
            v.tensor_scalar(c(11), c(10), 127.0, None, OP.mult).then_inc(sem("p1v"), 1); inc("p1v")
            v.wait_ge(sem("p1v"), val("p1v"))
            v.wait_ge(sem("p0v"), MEFF)
            v.tensor_scalar(xnt, xnt, c(11), meff[:, 0:1],
                            OP.mult, OP.add).then_inc(sem("p1q1"), 1); inc("p1q1")
            if i >= 1:
                evac(i - 1)
        evac(NT - 1)
        v.wait_ge(sem("p1v"), val("p1v"))
        v.tensor_scalar(bias_exp[:], mask[:], -B0, None, OP.add).then_inc(sem("p1bias"), 1); inc("p1bias")
    on("vector", p1_vector)
    P1EV = val("p1ev")
    P1BIAS = val("p1bias")
    for _g in (g_identb, g_stt, g_xqb, g_xnb, g_xn_t, g_x_t):
        _g.__exit__(None, None, None)
    g_psB.__exit__(None, None, None)
    psD = E(nc.psum_tensor("psD", [128, 512], F32))



    WABS = W1CH + W2CH

    def wmean_vec(v):
        v.wait_ge(sem("wabs"), WABS)
        v.reduce_sum(tiny[:, 2:3], wsum1[:], axis=AX).then_inc(sem("wmv"), 1); inc("wmv")
        v.reduce_sum(tiny[:, 3:4], wsum2[:], axis=AX).then_inc(sem("wmv"), 1); inc("wmv")
    on("vector", wmean_vec)
    WMV1 = val("wmv")

    def wmean_tensor(t):
        t.wait_ge(sem("p0v"), MEFF)
        t.wait_ge(sem("wmv"), WMV1)
        t.matmul(psC[0:1, 0:1], tiny[:, 2:3], onesc[:], start=True, stop=True)
        mm = t.matmul(psC[0:1, 1:2], tiny[:, 3:4], onesc[:], start=True, stop=True)
        mm.then_inc(sem("wmt"), 1); inc("wmt")
    on("tensor", wmean_tensor)
    WMT1 = val("wmt")

    def wmean_vec2(v):
        v.wait_ge(sem("wmt"), WMT1)
        v.tensor_scalar(tiny[0:1, 4:6], psC[0:1, 0:2], 1.0 / (C * DF), 1e-5,
                        OP.mult, OP.max).then_inc(sem("wmv"), 1); inc("wmv")
        v.wait_ge(sem("wmv"), val("wmv"))
        v.reciprocal(tiny[0:1, 6:8], tiny[0:1, 4:6]).then_inc(sem("wmv"), 1); inc("wmv")
    on("vector", wmean_vec2)
    WMV2 = val("wmv")

    def wmean_tensor2(t):
        t.wait_ge(sem("wmv"), WMV2)
        t.matmul(psC[:, 8:9], onesr[0:1, 0:128], tiny[0:1, 6:7], start=True, stop=True)
        t.matmul(psC[:, 9:10], onesr[0:1, 0:128], tiny[0:1, 7:8], start=True, stop=True)
        mm = t.matmul(psC[:, 10:11], onesr[0:1, 0:128], tiny[0:1, 5:6],
                      start=True, stop=True)
        mm.then_inc(sem("wmt"), 1); inc("wmt")
    on("tensor", wmean_tensor2)
    WMT2 = val("wmt")

    def wmean_vec3(v):
        v.wait_ge(sem("wmt"), WMT2)
        v.tensor_copy(sw1c[:], psC[:, 8:9]).then_inc(sem("wmv"), 1); inc("wmv")
        v.tensor_copy(sw2c[:], psC[:, 9:10]).then_inc(sem("wmv"), 1); inc("wmv")
        v.tensor_copy(mw2c[:], psC[:, 10:11]).then_inc(sem("wmv"), 1); inc("wmv")
        v.wait_ge(sem("wmv"), val("wmv"))
        v.reciprocal(tiny[:, 8:9], sw1c[:]).then_inc(sem("wmv"), 1); inc("wmv")
        v.wait_ge(sem("wmv"), val("wmv"))
        v.tensor_scalar(cws[:], cwb[:], tiny[:, 8:9], None,
                        OP.mult).then_inc(sem("wmv"), 1); inc("wmv")
    on("vector", wmean_vec3)
    WSC = val("wmv")

    g_wqb = nc.sbuf_tensor("wqb", [128, 2 * 1024], BF16)
    wqb = g_wqb.__enter__()

    NW = W1CH + W2CH

    def qdma2(ch, eng):
        src = w1_src(ch) if ch < W1CH else w2_src(ch - W1CH)
        eng.dma_start(out=wf[:, (ch % 4) * 1024:(ch % 4 + 1) * 1024],
                      in_=src).then_inc(sem(f"ql{ch % 4}"), 16)
        inc(f"ql{ch % 4}", 16)

    def wq_act1(s, ch):
        scl = sw1c if ch < W1CH else sw2c
        s.wait_ge(sem(f"ql{ch % 4}"), 16 * (ch // 4 + 1))
        s.activation(out=wf[:, (ch % 4) * 1024:(ch % 4 + 1) * 1024],
                     in_=wf[:, (ch % 4) * 1024:(ch % 4 + 1) * 1024],
                     func=AF.Identity,
                     bias=meff[:, 0:1], scale=scl[:, 0:1]).then_inc(sem("wqs"), 1)
        inc("wqs")

    def wq_act2(s, ch):
        s.wait_ge(sem("wqs"), 2 * ch + 1)
        s.activation(out=wf[:, (ch % 4) * 1024:(ch % 4 + 1) * 1024],
                     in_=wf[:, (ch % 4) * 1024:(ch % 4 + 1) * 1024],
                     func=AF.Identity,
                     bias=meffneg[:, 0:1]).then_inc(sem("wqs"), 1)
        inc("wqs")

    def wq_store(q):
        for ch in range(W1CH + W2CH):
            b = ch % 2
            q.wait_ge(sem("wqv"), ch + 1)
            if ch < W1CH:
                r, h = ch // 4, ch % 4
                dst = w1q_d[h * 8:(h + 1) * 8, :, r * 128:(r + 1) * 128].rearrange(
                    "j p m -> p j m")
                src = wqb[:, b * 1024:(b + 1) * 1024].rearrange("p (j m) -> p j m", j=8)
            else:
                dst = w2q_d[(ch - W1CH) * 128:(ch - W1CH + 1) * 128, :]
                src = wqb[:, b * 1024:(b + 1) * 1024]
            q.dma_start(out=dst, in_=src).then_inc(sem(f"wqst{b}"), 16)
            inc(f"wqst{b}", 16)
    on("sync", wq_store)
    WQST0 = val("wqst0")
    WQST1 = val("wqst1")

    w1q_sb = E(nc.sbuf_tensor("w1q_sb", [128, 2 * 1024], BF16, side="right"))
    w2q_sb = E(nc.sbuf_tensor("w2q_sb", [128, 4 * 512], BF16, side="right"))
    g_qT = nc.sbuf_tensor("qroll", [128, 2 * 2048], BF16, side="right"); qT = g_qT.__enter__()
    g_kT = nc.sbuf_tensor("kroll", [128, 2 * 2048], BF16, side="right"); kT = g_kT.__enter__()
    qTv = qT.ap().rearrange("p (s t) -> p s t", s=2)
    kTv = kT.ap().rearrange("p (s t) -> p s t", s=2)
    g_v_sb = nc.sbuf_tensor("vroll", [128, 2 * NT * 260], BF16, side="right"); v_sb = g_v_sb.__enter__()
    v_v = v_sb.ap().rearrange("p (s t h d) -> p s t h d", s=2, t=NT, h=4)
    g_probs = nc.sbuf_tensor("probs", [128, 4 * 1024], BF16, side="right"); probs = g_probs.__enter__()
    g_wblk = nc.sbuf_tensor("wblk", [128, 2 * 1024], BF16); wblk = g_wblk.__enter__()
    g_wvq = nc.sbuf_tensor("wvq", [128, 2 * 2048], BF16); wvq = g_wvq.__enter__()

    prods = ([("q", 0, qb) for qb in range(4)] + [("k", 0, qb) for qb in range(4)]
             + [("v", 0, ti) for ti in range(NT)])
    PROLOG = len(prods)
    seg_prod = {}
    for hp in range(NHP):
        lst = []
        if hp < NHP - 1:
            lst += [("q", hp + 1, qb) for qb in range(4)]
            lst += [("k", hp + 1, qb) for qb in range(4)]
        if hp < 6:
            lst += [("v", hp // 2 + 1, (hp % 2) * 8 + m) for m in range(8)]
        seg_prod[hp] = lst
        prods = prods + lst
    pidx = {item: i for i, item in enumerate(prods)}
    seg_sched = {}
    for hp in range(NHP):
        chunks = [(item, half) for item in seg_prod[hp] for half in (0, 1)]
        mp = {}
        for m, ch_ in enumerate(chunks):
            mp.setdefault(m * 64 // max(len(chunks), 1), []).append(ch_)
        seg_sched[hp] = mp
    NQK, NV = {}, {}
    _nq = _nv = 0
    for i, item in enumerate(prods):
        if item[0] == "v":
            _nv += 1
        else:
            _nq += 1
        NQK[i], NV[i] = _nq, _nv
    last_k = {hp: pidx[("k", hp, 3)] for hp in range(NHP)}

    def p23_gload(g):
        g.dma_start(out=bvb[:], in_=bv_d[:]).then_inc(sem("p2bv"), 16)
        inc("p2bv", 16)
        g.wait_ge(sem("p1xq"), NT)
        g.wait_ge(sem("p1tpm"), 2 * NT)
        g.wait_ge(sem("p1sq"), NT)
        g.dma_start(out=wblk[:, 0:1024], in_=wq_d[0]).then_inc(sem("lq"), 16); inc("lq", 16)
        g.dma_start(out=wblk[:, 1024:2048], in_=wk_d[0]).then_inc(sem("lk"), 16); inc("lk", 16)
        g.dma_start(out=wvq[:, 0:2048], in_=wv_d[0]).then_inc(sem("lv0"), 16); inc("lv0", 16)
        g.dma_start(out=wvq[:, 2048:4096], in_=wv_d[1]).then_inc(sem("lv1"), 16); inc("lv1", 16)
        for hp in range(1, NHP):
            g.wait_ge(sem("qkp"), 8 * (hp - 1) + 4)
            g.dma_start(out=wblk[:, 0:1024], in_=wq_d[hp]).then_inc(sem("lq"), 16); inc("lq", 16)
            g.wait_ge(sem("qkp"), 8 * hp)
            g.dma_start(out=wblk[:, 1024:2048], in_=wk_d[hp]).then_inc(sem("lk"), 16); inc("lk", 16)
            if hp == 2:
                g.wait_ge(sem("vmm"), 16)
                g.dma_start(out=wvq[:, 0:2048], in_=wv_d[2]).then_inc(sem("lv0"), 16); inc("lv0", 16)
            if hp == 4:
                g.wait_ge(sem("vmm"), 32)
                g.dma_start(out=wvq[:, 2048:4096], in_=wv_d[3]).then_inc(sem("lv1"), 16); inc("lv1", 16)
    on("gpsimd", p23_gload)

    def p23_tensor(t):
        t.wait_ge(sem("p0v"), MEFF)
        t.wait_ge(sem("p1ev"), P1EV)
        t.wait_ge(sem("p1bias"), P1BIAS)

        def prod_block(item, half=None):
            kind, a, b = item
            i = pidx[item]
            ccs = range(NCH) if half is None else range(half * 4, half * 4 + 4)
            if i >= 1 and half != 1:
                t.wait_ge(sem("pde"), i)
            if kind in ("q", "k"):
                if half != 1:
                    t.wait_ge(sem("lq" if kind == "q" else "lk"), 16 * (a + 1))
                off = 0 if kind == "q" else 1024
                for cc in ccs:
                    mm = t.matmul(psD[:, 0:512],
                                  wblk[:, off + cc * 128:off + (cc + 1) * 128],
                                  xnTv[:, cc, b * 512:(b + 1) * 512],
                                  start=(cc == 0), stop=(cc == NCH - 1))
                if half != 0:
                    mm.then_inc(sem("qkp"), 1)
                    inc("qkp")
            else:
                qv, ti = a, b
                if half != 1:
                    t.wait_ge(sem(f"lv{qv % 2}"), 16 * (qv // 2 + 1))
                for cc in ccs:
                    t.matmul(psD[:, 0:256],
                             xnTv[:, cc, ti * 128:(ti + 1) * 128],
                             wvq[:, (qv % 2) * 2048 + cc * 256:(qv % 2) * 2048 + (cc + 1) * 256],
                             start=(cc == 0), stop=False)
                if half != 0:
                    if qv == 0 and ti == 0:
                        t.wait_ge(sem("p2bv"), 16)
                    mm = t.matmul(psD[:, 0:256], onesb[0:1, 0:128],
                                  bvb[0:1, qv * 256:(qv + 1) * 256], start=False, stop=True)
                    mm.then_inc(sem("vmm"), 1)
                    inc("vmm")

        def emit_ctx(jj):
            itj, kcj = divmod(jj, NT)
            hpj = itj // NQB
            qvj = hpj // 2
            laj = 2 * (hpj % 2)
            if jj == 0:
                t.wait_ge(sem("vone"), 2)
            if kcj == 0:
                t.wait_ge(sem("p3ev2"), 3 * itj)
            t.wait_ge(sem("p3exp"), jj + 1)
            t.wait_ge(sem("pde"), pidx[("v", qvj, kcj)] + 1)
            pslot = jj % 4
            t.matmul(psC[0:65, 0:512],
                     v_v[:, qvj % 2, kcj, laj, 0:65],
                     probs[:, pslot * 1024:pslot * 1024 + 512],
                     start=(kcj == 0), stop=(kcj == NT - 1))
            mm = t.matmul(psC[0:65, 512:1024],
                          v_v[:, qvj % 2, kcj, laj + 1, 0:65],
                          probs[:, pslot * 1024 + 512:pslot * 1024 + 1024],
                          start=(kcj == 0), stop=(kcj == NT - 1))
            mm.then_inc(sem("p3pf"), 1)
            inc("p3pf")
            if kcj == NT - 1:
                t.wait_ge(sem("p3ev2"), 3 * itj + 3)
                if itj > 0:
                    t.wait_ge(sem("p3nv"), itj)
                t.matmul(psC[0:64, 1024:1536], onesr[0:1, 0:64], denrow[0:1, 0:512],
                         start=True, stop=True)
                mm = t.matmul(psC[64:128, 1024:1536], onesr[0:1, 0:64],
                              denrow[0:1, 512:1024], start=True, stop=True)
                mm.then_inc(sem("p3nb"), 1)
                inc("p3nb")

        for item in prods[:PROLOG]:
            prod_block(item)
        NJ3 = NHP * NQB * NT
        for j in range(NJ3):
            it, kc = divmod(j, NT)
            hp, qb = divmod(it, NQB)
            jl = j - hp * 64
            if jl == 0:
                t.wait_ge(sem("pde"), last_k[hp] + 1)
            pp = kc % 2
            sA = psA[:, pp * 1024:pp * 1024 + 512]
            sB = psA[:, pp * 1024 + 512:pp * 1024 + 1024]
            if j >= 2:
                t.wait_ge(sem("p3exp"), j - 1)
            t.matmul(sA, kTv[0:64, hp % 2, kc * 128:(kc + 1) * 128],
                     qTv[0:64, hp % 2, qb * 512:(qb + 1) * 512],
                     start=True, stop=True, tile_position=(0, 0))
            mm = t.matmul(sB, kTv[64:128, hp % 2, kc * 128:(kc + 1) * 128],
                          qTv[64:128, hp % 2, qb * 512:(qb + 1) * 512],
                          start=True, stop=True, tile_position=(64, 0))
            mm.then_inc(sem("p3s"), 1)
            inc("p3s")
            if j >= 1:
                emit_ctx(j - 1)
            for item, half in seg_sched[hp].get(jl, []):
                prod_block(item, half)
        emit_ctx(NJ3 - 1)
    on("tensor", p23_tensor)
    P3PF = val("p3pf")

    def p23_scalar(s):
        s.wait_ge(sem("wmv"), WSC)
        s.wait_ge(sem("p1bias"), P1BIAS)
        for ch in range(4):
            qdma2(ch, s)
        NJ3 = NHP * NQB * NT
        for j in range(NJ3):
            if j % 8 == 4:
                chl = j // 8 + 3
                if 4 <= chl < NW:
                    s.wait_ge(sem("wqv"), chl - 3)
                    qdma2(chl, s)
            pp = j % 2
            pslot = j % 4
            kcn = j % NT
            s.wait_ge(sem("p3s"), j + 1)
            if j >= 4:
                s.wait_ge(sem("p3pf"), j - 3)
            s.activation(out=probs[:, pslot * 1024:(pslot + 1) * 1024],
                         in_=psA[:, pp * 1024:(pp + 1) * 1024],
                         func=AF.Exp, scale=0.125,
                         bias=bias_exp[:, kcn:kcn + 1]).then_inc(sem("p3exp"), 1)
            inc("p3exp")
            if j % NT == NT - 1:
                itb = j // NT
                hpb, qbb = divmod(itb, NQB)
                s.wait_ge(sem("p3pf"), (itb + 1) * NT)
                s.activation(out=ctxv[0:64, hpb, qbb * 512:(qbb + 1) * 512],
                             in_=psC[0:64, 0:512],
                             func=AF.Identity).then_inc(sem("p3ev2"), 1)
                inc("p3ev2")
            if j % 8 == 2 and j // 8 < NW:
                wq_act1(s, j // 8)
            if j % 8 == 5 and j // 8 < NW:
                wq_act2(s, j // 8)
    on("scalar", p23_scalar)

    def p23_vec(v):
        v.memset(v_v[:, 0, :, :, 64:65], 1.0).then_inc(sem("vone"), 1); inc("vone")
        v.memset(v_v[:, 1, :, :, 64:65], 1.0).then_inc(sem("vone"), 1); inc("vone")

        def wclip(ch):
            b4 = ch % 4
            b = ch % 2
            v.wait_ge(sem("wqs"), 2 * ch + 2)
            if ch >= 2:
                v.wait_ge(sem(f"wqst{ch % 2}"), 16 * (ch // 2))
            v.tensor_scalar(wqb[:, b * 1024:(b + 1) * 1024],
                            wf[:, b4 * 1024:(b4 + 1) * 1024],
                            -1.0, 1.0, OP.max, OP.min).then_inc(sem("wqv"), 1); inc("wqv")

        def evac_prod(item):
            kind, a, b = item
            i = pidx[item]
            if kind == "v":
                v.wait_ge(sem("vmm"), NV[i])
                v.tensor_copy(v_v[:, a % 2, b, 0:4, 0:64],
                              psD[:, 0:256].rearrange("p (h d) -> p h d", h=4)
                              ).then_inc(sem("pde"), 1)
            else:
                v.wait_ge(sem("qkp"), NQK[i])
                dstv, bc = (qTv, bqc) if kind == "q" else (kTv, bkc)
                v.tensor_scalar(dstv[:, a % 2, b * 512:(b + 1) * 512],
                                psD[:, 0:512],
                                bc[:, a:a + 1], None, OP.add).then_inc(sem("pde"), 1)
            inc("pde")

        def ctx_norm(it):
            hp, qb = divmod(it, NQB)
            v.wait_ge(sem("p3nb"), it + 1)
            v.reciprocal(rtile[:], psC[:, 1024:1536]).then_inc(sem("p3nv"), 1); inc("p3nv")
            v.wait_ge(sem("p3nv"), val("p3nv"))
            v.tensor_tensor(ctxv[:, hp, qb * 512:(qb + 1) * 512],
                            ctxv[:, hp, qb * 512:(qb + 1) * 512],
                            rtile[:], OP.mult).then_inc(sem("p3tt"), 1); inc("p3tt")

        for item in prods[:PROLOG]:
            evac_prod(item)
        NJ3 = NHP * NQB * NT
        for j in range(NJ3):
            it, kc = divmod(j, NT)
            hp, qb = divmod(it, NQB)
            jl = j - hp * 64
            for item, half in seg_sched[hp].get(jl, []):
                if half == 1:
                    evac_prod(item)
            if j % 8 == 1 and 1 <= j // 8 <= NW:
                wclip(j // 8 - 1)
            if kc == 4 and it >= 1:
                ctx_norm(it - 1)
            if kc == NT - 1:
                v.wait_ge(sem("p3pf"), (it + 1) * NT)
                v.tensor_copy(ctxv[64:128, hp, qb * 512:(qb + 1) * 512],
                              psC[0:64, 512:1024]).then_inc(sem("p3ev2"), 1); inc("p3ev2")
                v.tensor_copy(denrow[0:1, :],
                              psC[64:65, 0:1024]).then_inc(sem("p3ev2"), 1); inc("p3ev2")
        ctx_norm(NHP * NQB - 1)
        wclip(NW - 1)
    on("vector", p23_vec)
    P3TT = val("p3tt")

    for _g in (g_probs, g_v_sb, g_kT, g_qT):
        _g.__exit__(None, None, None)
    for _g in (g_wvq, g_wblk, g_wqb, g_wf, g_xnT):
        _g.__exit__(None, None, None)
    g_wo = nc.sbuf_tensor("wo_bf", [128, 8 * 1024], BF16); wo_bf = g_wo.__enter__()
    g_yat = nc.sbuf_tensor("y_attn", [128, NT * 1024], BF16, side="right"); y_attn = g_yat.__enter__()
    y_av = y_attn.ap().rearrange("p (t m) -> p t m", t=NT)

    def p35_gload(g):
        g.wait_ge(sem("qkp"), 8 * NHP)
        g.wait_ge(sem("vmm"), 4 * NT)
        g.wait_ge(sem("wqv"), NW)
        g.wait_ge(sem("wqst0"), WQST0)
        g.wait_ge(sem("wqst1"), WQST1)
        for cc in range(NCH):
            g.dma_start(out=wo_bf[:, cc * 1024:(cc + 1) * 1024],
                        in_=wo_d[cc * 128:(cc + 1) * 128, :]).then_inc(sem("p3wo"), 16)
            inc("p3wo", 16)
    on("gpsimd", p35_gload)
    P3WO = val("p3wo")

    def p35_tensor(t):
        t.wait_ge(sem("p3tt"), P3TT)
        t.wait_ge(sem("p3wo"), P3WO)
        for ti in range(NT):
            for co in range(2):
                n = ti * 2 + co
                slot = n % 3
                ps = psA[:, slot * 512:(slot + 1) * 512]
                if n >= 3:
                    t.wait_ge(sem("p35oev"), n - 2)
                for cc in range(NCH):
                    t.matmul(ps, ctxv[:, cc, ti * 128:(ti + 1) * 128],
                             wo_bf[:, cc * 1024 + co * 512:cc * 1024 + (co + 1) * 512],
                             start=(cc == 0), stop=False)
                mm = t.matmul(ps, onesr[0:1, 0:128], bor[0:1, co * 512:(co + 1) * 512],
                              start=False, stop=True)
                mm.then_inc(sem("p35o"), 1)
                inc("p35o")
    on("tensor", p35_tensor)

    def p35_vec(v):
        for ti in range(NT):
            for co in range(2):
                n = ti * 2 + co
                slot = n % 3
                v.wait_ge(sem("p35o"), n + 1)
                v.tensor_copy(y_av[:, ti, co * 512:(co + 1) * 512],
                              psA[:, slot * 512:(slot + 1) * 512]).then_inc(sem("p35oev"), 1); inc("p35oev")
    on("vector", p35_vec)
    P35OEV = val("p35oev")

    for _g in (g_wo, g_rr, g_dr, g_ctx, g_bvb, g_onesb, g_bor):
        _g.__exit__(None, None, None)
    h_act = E(nc.sbuf_tensor("h_act", [128, NJC * 512], F32))
    h_v = h_act.ap().rearrange("p (j t) -> p j t", j=NJC)
    hq = E(nc.sbuf_tensor("hq", [128, NJC * 512], BF16))
    hq_v = hq.ap().rearrange("p (j t) -> p j t", j=NJC)
    u_sb = E(nc.sbuf_tensor("u_sb", [128, 3 * 512], F32))
    qscr = E(nc.sbuf_tensor("qscr", [128, 2 * 512], F32))
    qscr2 = E(nc.sbuf_tensor("qscr2", [128, 512], F32))
    shb_sb = E(nc.sbuf_tensor("shb_sb", [128, 512], F32))
    amax1 = E(nc.sbuf_tensor("amax1", [128, 512], F32))
    amax2 = E(nc.sbuf_tensor("amax2", [128, 512], F32))
    xr_t = E(nc.sbuf_tensor("xr_t", [128, 4 * 512], F32))
    out_t = E(nc.sbuf_tensor("out_t", [128, 2 * 1024], F32))

    def w1q_src(jc):
        return w1q_d[jc]

    P35OT = val("p35o")

    def p4_scalar(s):
        s.wait_ge(sem("wqst0"), WQST0)
        s.wait_ge(sem("wqst1"), WQST1)

        def op1_one(tb, jc):
            r = tb * NJC + jc
            pslot = jc % 2
            ub = r % 3
            ps = psA[:, pslot * 1024:pslot * 1024 + 514]
            s.wait_ge(sem("p4m1"), r + 1)
            if r >= 3:
                s.wait_ge(sem("p4si"), r - 2)
            s.activation(out=u_sb[:, ub * 512:(ub + 1) * 512], in_=ps[:, 0:512],
                         func=AF.Identity, scale=cws[:, jc * 3 + 0:jc * 3 + 1],
                         bias=0.0).then_inc(sem("p4c1"), 1)
            inc("p4c1")

        def silu_one(tb, jc):
            r = tb * NJC + jc
            ub = r % 3
            s.wait_ge(sem("p4cv"), 2 * r + 2)
            if tb >= 1:
                s.wait_ge(sem("p4q"), (tb - 1) * NJC + jc + 1)
            s.activation(out=h_v[:, jc, 0:512],
                         in_=u_sb[:, ub * 512:(ub + 1) * 512],
                         func=AF.Silu, bias=cbc[:, jc:jc + 1]).then_inc(sem("p4si"), 1)
            inc("p4si")

        def round_one(tb, jc):
            r = tb * NJC + jc
            qs = r % 2
            s.wait_ge(sem("p4qa"), r + 1)
            if r >= 1:
                s.wait_ge(sem("p4q"), r)
            s.activation(out=qscr2[:],
                         in_=qscr[:, qs * 512:(qs + 1) * 512],
                         func=AF.Identity, bias=meff[:, 0:1]).then_inc(sem("p4qb"), 1)
            inc("p4qb")
            s.wait_ge(sem("p4qb"), val("p4qb"))
            s.activation(out=hq_v[:, jc, 0:512],
                         in_=qscr2[:],
                         func=AF.Identity, bias=meffneg[:, 0:1]).then_inc(sem("p4q"), 1)
            inc("p4q")

        def silu_block(tb):
            for jc in range(NJC):
                op1_one(tb, jc)
                if jc >= 2:
                    silu_one(tb, jc - 2)
                if tb >= 1:
                    round_one(tb - 1, jc)
            silu_one(tb, NJC - 2)
            silu_one(tb, NJC - 1)

        def w2q_block(tb):
            for co in range(2):
                for jc in range(NJC):
                    rr = (tb * 2 + co) * NJC + jc
                    b2 = rr % 4
                    if b2 >= 2:
                        continue
                    if rr >= 4:
                        s.wait_ge(sem("p4m2c"), rr - 3)
                    s.dma_start(out=w2q_sb[:, b2 * 512:(b2 + 1) * 512],
                                in_=w2q_d[jc * 128:(jc + 1) * 128,
                                          co * 512:(co + 1) * 512]).then_inc(
                                    sem(f"p4w2l{b2}"), 16)
                    inc(f"p4w2l{b2}", 16)

        silu_block(0)
        for tb in range(1, NTB):
            silu_block(tb)
            w2q_block(tb - 1)
        for jc in range(NJC):
            round_one(NTB - 1, jc)
            for co in range(2):
                idx = 2 * jc + co
                b2 = idx % 4
                if b2 >= 2:
                    continue
                s.wait_ge(sem("p4m2c"), 192 + idx - 3)
                s.dma_start(out=w2q_sb[:, b2 * 512:(b2 + 1) * 512],
                            in_=w2q_d[jc * 128:(jc + 1) * 128,
                                      co * 512:(co + 1) * 512]).then_inc(
                                sem(f"p4w2l{b2}"), 16)
                inc(f"p4w2l{b2}", 16)
    on("scalar", p4_scalar)

    def p4_tensor(t):
        t.wait_ge(sem("p1z"), 2)
        t.wait_ge(sem("p35oev"), P35OEV)

        def mm1_block(tb, jc0, jc1):
            for jc in range(jc0, jc1):
                r = tb * NJC + jc
                b = r % 2
                pslot = jc % 2
                ps = psA[:, pslot * 1024:pslot * 1024 + 514]
                t.wait_ge(sem(f"p4w1l{b}"), 16 * (r // 2 + 1))
                if r >= 2:
                    t.wait_ge(sem("p4cv"), 2 * (r - 1))
                for cc in range(NCH):
                    t.matmul(ps[:, 0:512],
                             w1q_sb[:, b * 1024 + cc * 128:b * 1024 + (cc + 1) * 128],
                             xqTv[:, cc, tb * 512:tb * 512 + 512],
                             start=(cc == 0), stop=(cc == NCH - 1))
                for cc in range(NCH):
                    mm = t.matmul(ps[:, 512:514],
                                  w1q_sb[:, b * 1024 + cc * 128:b * 1024 + (cc + 1) * 128],
                                  xqTv[:, cc, tb * 512 + 512:tb * 512 + 514],
                                  start=(cc == 0), stop=(cc == NCH - 1))
                mm.then_inc(sem("p4m1"), 1)
                inc("p4m1")

        def chain_block(tb):
            t.wait_ge(sem("p4sh"), 8 * tb + 1)
            for j in range(4):
                ao = AMX_OFF[j]
                mm = t.transpose(psA[:, ao:ao + 128],
                                 amax1[:, j * 128:(j + 1) * 128], identf[:])
            mm.then_inc(sem("p4sh"), 1)
            inc("p4sh")
            t.wait_ge(sem("p4sh"), 8 * tb + 4)
            for j in range(4):
                ao = AMX_OFF[j]
                mm = t.transpose(psA[0:1, ao:ao + 128],
                                 sh_cols[:, tb * 4 + j:tb * 4 + j + 1], identf[:])
            mm.then_inc(sem("p4sh"), 1)
            inc("p4sh")
            t.wait_ge(sem("p4sh"), 8 * tb + 6)
            for j in range(4):
                ao = AMX_OFF[j]
                mm = t.matmul(psA[:, ao:ao + 128],
                              onesr[0:1, 0:128], trow[0:1, j * 128:(j + 1) * 128],
                              start=True, stop=True)
            mm.then_inc(sem("p4sh"), 1)
            inc("p4sh")

        def mm2_block(tb):
            for co in range(2):
                for jc in range(NJC):
                    rr = (tb * 2 + co) * NJC + jc
                    b2 = rr % 4
                    t.wait_ge(sem(f"p4w2l{b2}"), 16 * (rr // 4 + 1))
                    t.wait_ge(sem("p4q"), tb * NJC + jc + 1)
                    if jc == 0:
                        t.wait_ge(sem("p4fin"), (tb * 2 + co) * 4)
                    for tt in range(4):
                        psm2 = psD[:, 0:512] if tt == 3 else psC[:, tt * 512:(tt + 1) * 512]
                        mm = t.matmul(psm2,
                                      hq_v[:, jc, tt * 128:(tt + 1) * 128],
                                      w2q_sb[:, b2 * 512:(b2 + 1) * 512],
                                      start=(jc == 0), stop=(jc == NJC - 1))
                    mm.then_inc(sem("p4m2c"), 1)
                    inc("p4m2c")

        def mm2_block_last(tb):
            t.wait_ge(sem("p4sh"), 8 * tb + 8)
            t.wait_ge(sem("p4cv"), 2 * (tb + 1) * NJC)
            t.wait_ge(sem("p4fin"), tb * 2 * 4)
            for jc in range(NJC):
                t.wait_ge(sem("p4q"), tb * NJC + jc + 1)
                for co in range(2):
                    idx = 2 * jc + co
                    b2 = idx % 4
                    t.wait_ge(sem(f"p4w2l{b2}"), 16 * (48 + idx // 4 + 1))
                    for tt in range(4):
                        if co == 0:
                            psm2 = psD[:, 0:512] if tt == 3 else psC[:, tt * 512:(tt + 1) * 512]
                        else:
                            psm2 = psA[:, tt * 512:(tt + 1) * 512]
                        mm = t.matmul(psm2,
                                      hq_v[:, jc, tt * 128:(tt + 1) * 128],
                                      w2q_sb[:, b2 * 512:(b2 + 1) * 512],
                                      start=(jc == 0), stop=(jc == NJC - 1))
                    mm.then_inc(sem("p4m2c"), 1)
                    inc("p4m2c")

        mm1_block(0, 0, NJC)
        for tb in range(1, NTB):
            mm1_block(tb, 0, 2)
            chain_block(tb - 1)
            mm1_block(tb, 2, NJC)
            mm2_block(tb - 1)
        chain_block(NTB - 1)
        mm2_block_last(NTB - 1)
    on("tensor", p4_tensor)

    def p4_vec(v):
        v.wait_ge(sem("wmv"), WSC)

        def chain_v_block(tb):
            v.wait_ge(sem("p4sh"), 8 * tb + 2)
            for j in range(4):
                ti = tb * 4 + j
                ao = AMX_OFF[j]
                v.reduce_max(tiny[:, 9:10], psA[:, ao:ao + 128],
                             axis=AX).then_inc(sem("p4v"), 1); inc("p4v")
                v.wait_ge(sem("p4v"), val("p4v"))
                v.tensor_scalar(tiny[:, 10:11], tiny[:, 9:10], 1e-5, None,
                                OP.max).then_inc(sem("p4v"), 1); inc("p4v")
                v.wait_ge(sem("p4v"), val("p4v"))
                v.tensor_scalar(tiny[:, 11:12], tiny[:, 10:11], 1.0 / 127.0, None,
                                OP.mult).then_inc(sem("p4v"), 1); inc("p4v")
                v.wait_ge(sem("p4v"), val("p4v"))
                v.tensor_tensor(shc[:, ti:ti + 1], tiny[:, 11:12], mw2c[:],
                                OP.mult).then_inc(sem("p4v"), 1); inc("p4v")
                v.reciprocal(tiny[:, 12:13], tiny[:, 10:11]).then_inc(sem("p4v"), 1); inc("p4v")
                v.wait_ge(sem("p4v"), val("p4v"))
                v.tensor_scalar(sh_cols[:, ti:ti + 1], tiny[:, 12:13], 127.0, None,
                                OP.mult).then_inc(sem("p4v"), 1); inc("p4v")
            v.wait_ge(sem("p4v"), val("p4v"))
            v.engine_nop().then_inc(sem("p4sh"), 1)
            inc("p4sh")
            v.engine_nop().then_inc(sem("p4sh"), 1)
            inc("p4sh")
            v.wait_ge(sem("p4sh"), 8 * tb + 5)
            v.tensor_copy(trow[0:1, 0:256], psA[0:1, 640:896])
            v.tensor_copy(trow[0:1, 256:512], psA[0:1, 1664:1920]).then_inc(sem("p4sh"), 1)
            inc("p4sh")
            v.wait_ge(sem("p4sh"), 8 * tb + 7)
            v.tensor_copy(shb_sb[:, 0:256], psA[:, 640:896])
            v.tensor_copy(shb_sb[:, 256:512], psA[:, 1664:1920]).then_inc(sem("p4sh"), 1)
            inc("p4sh")

        def finals_co(tb, co):
            if True:
                k_ = tb * 2 + co
                if tb == NTB - 1:
                    v.wait_ge(sem("p4m2c"), 192 + 63 + co)
                else:
                    v.wait_ge(sem("p4m2c"), (k_ + 1) * NJC)
                for tt in range(4):
                    ti = tb * 4 + tt
                    osl = out_t[:, (tt % 2) * 1024 + co * 512:(tt % 2) * 1024 + (co + 1) * 512]
                    v.wait_ge(sem(f"p4xl{tt}"), 16 * (k_ + 1))
                    v.wait_ge(sem("p35oev"), P35OEV)
                    s_cls = (tt % 2) * 2 + co
                    m_ = k_ // 2
                    if tt >= 2:
                        v.wait_ge(sem(f"p4st{s_cls}"), 16 * (2 * m_ + 1))
                    elif m_ >= 1:
                        v.wait_ge(sem(f"p4st{s_cls}"), 16 * (2 * m_))
                    v.tensor_tensor(
                        osl,
                        xr_t[:, tt * 512:(tt + 1) * 512],
                        y_av[:, ti, co * 512:(co + 1) * 512],
                        OP.add).then_inc(sem("p4f1"), 1); inc("p4f1")
                    v.wait_ge(sem("p4f1"), val("p4f1"))
                    if tb == NTB - 1 and co == 1:
                        psff = psA[:, tt * 512:(tt + 1) * 512]
                    elif tt == 3:
                        psff = psD[:, 0:512]
                    else:
                        psff = psC[:, tt * 512:(tt + 1) * 512]
                    v.scalar_tensor_tensor(
                        osl,
                        psff,
                        shc[:, ti:ti + 1],
                        osl,
                        OP.mult, OP.add).then_inc(sem("p4fin"), 1); inc("p4fin")

        hT = h_act.ap().rearrange("p (j t) -> p t j", j=NJC)

        def amax_piece(tb, k):
            v.wait_ge(sem("p4si"), tb * NJC + 4 * k + 4)
            v.wait_ge(sem("p4fo"), val("p4fo"))
            if k == 0:
                if tb >= 1:
                    v.wait_ge(sem("p4sh"), 8 * (tb - 1) + 2)
                v.reduce_max(amax1[:], hT[:, :, 0:4], axis=AX,
                             apply_absolute_value=True).then_inc(sem("p4fo"), 1)
                inc("p4fo")
                return
            v.reduce_max(amax2[:], hT[:, :, 4 * k:4 * k + 4], axis=AX,
                         apply_absolute_value=True).then_inc(sem("p4fo"), 1)
            inc("p4fo")
            v.wait_ge(sem("p4fo"), val("p4fo"))
            mm = v.tensor_max(amax1[:], amax1[:], amax2[:])
            if k == 7:
                mm.then_inc(sem("p4sh"), 1)
                inc("p4sh")
            else:
                mm.then_inc(sem("p4fo"), 1)
                inc("p4fo")

        def conv_one(tb, jc):
            r = tb * NJC + jc
            pslot = jc % 2
            ub = r % 3
            ps = psA[:, pslot * 1024:pslot * 1024 + 514]
            v.wait_ge(sem("p4m1"), r + 1)
            v.wait_ge(sem("p4c1"), r + 1)
            v.scalar_tensor_tensor(u_sb[:, ub * 512:(ub + 1) * 512],
                                   ps[:, 1:513], cws[:, jc * 3 + 1:jc * 3 + 2],
                                   u_sb[:, ub * 512:(ub + 1) * 512],
                                   OP.mult, OP.add).then_inc(sem("p4cv"), 1); inc("p4cv")
            v.wait_ge(sem("p4cv"), val("p4cv"))
            v.scalar_tensor_tensor(u_sb[:, ub * 512:(ub + 1) * 512],
                                   ps[:, 2:514], cws[:, jc * 3 + 2:jc * 3 + 3],
                                   u_sb[:, ub * 512:(ub + 1) * 512],
                                   OP.mult, OP.add).then_inc(sem("p4cv"), 1); inc("p4cv")

        def quant_one(tb, jc):
            r = tb * NJC + jc
            qs = r % 2
            v.wait_ge(sem("p4sh"), 8 * tb + 8)
            if r >= 2:
                v.wait_ge(sem("p4qb"), r - 1)
            v.tensor_tensor(qscr[:, qs * 512:(qs + 1) * 512], h_v[:, jc, 0:512],
                            shb_sb[:], OP.mult).then_inc(sem("p4qa"), 1); inc("p4qa")

        for jc in range(NJC):
            conv_one(0, jc)
            if jc >= 6 and (jc - 6) % 4 == 0:
                amax_piece(0, (jc - 6) // 4)
        amax_piece(0, 7)
        for tb in range(1, NTB):
            if tb >= 2:
                finals_co(tb - 2, 1)
            chain_v_block(tb - 1)
            for jc in range(NJC):
                quant_one(tb - 1, jc)
                conv_one(tb, jc)
                if jc >= 6 and (jc - 6) % 4 == 0:
                    amax_piece(tb, (jc - 6) // 4)
            amax_piece(tb, 7)
            finals_co(tb - 1, 0)
        finals_co(NTB - 2, 1)
        chain_v_block(NTB - 1)
        for jc in range(NJC):
            quant_one(NTB - 1, jc)
        finals_co(NTB - 1, 0)
        finals_co(NTB - 1, 1)
    on("vector", p4_vec)

    def p4_gsync(g):
        g.wait_ge(sem("wqst0"), WQST0)
        g.wait_ge(sem("wqst1"), WQST1)

        def w1q_block(tb):
            for jc in range(NJC):
                r = tb * NJC + jc
                b = r % 2
                if r >= 2:
                    g.wait_ge(sem("p4m1"), r - 1)
                g.dma_start(out=w1q_sb[:, b * 1024:(b + 1) * 1024],
                            in_=w1q_src(jc)).then_inc(sem(f"p4w1l{b}"), 16)
                inc(f"p4w1l{b}", 16)

        def x_block(k_):
            co = k_ % 2
            tb = k_ // 2
            if k_ == 0:
                g.wait_ge(sem("p35o"), P35OT)
            for tt in range(4):
                ti = tb * 4 + tt
                if k_ >= 1:
                    g.wait_ge(sem("p4f1"), (k_ - 1) * 4 + tt + 1)
                g.dma_start(out=xr_t[:, tt * 512:(tt + 1) * 512],
                            in_=x_d[ti * 128:(ti + 1) * 128,
                                    co * 512:(co + 1) * 512]).then_inc(sem(f"p4xl{tt}"), 16)
                inc(f"p4xl{tt}", 16)

        w1q_block(0)
        x_block(0)
        w1q_block(1)
        x_block(1)
        w1q_block(2)
        x_block(2)
        x_block(3)
        w1q_block(3)
        x_block(4)
        x_block(5)
        x_block(6)
        x_block(7)
    on("gpsimd", p4_gsync)

    def p4_sync(q):
        q.wait_ge(sem("wqst0"), WQST0)
        q.wait_ge(sem("wqst1"), WQST1)

        def w2q_sloads(order):
            for rr, jc, co in order:
                b2 = rr % 4
                if b2 < 2:
                    continue
                if rr >= 4:
                    q.wait_ge(sem("p4m2c"), rr - 3)
                q.dma_start(out=w2q_sb[:, b2 * 512:(b2 + 1) * 512],
                            in_=w2q_d[jc * 128:(jc + 1) * 128,
                                      co * 512:(co + 1) * 512]).then_inc(
                                sem(f"p4w2l{b2}"), 16)
                inc(f"p4w2l{b2}", 16)

        def stores_co(tb, co):
            for tt in range(4):
                ti = tb * 4 + tt
                n = (tb * 2 + co) * 4 + tt
                q.wait_ge(sem("p4fin"), n + 1)
                s_cls = (tt % 2) * 2 + co
                q.dma_start(out=out_d[ti * 128:(ti + 1) * 128,
                                      co * 512:(co + 1) * 512],
                            in_=out_t[:, (tt % 2) * 1024 + co * 512:(tt % 2) * 1024 + (co + 1) * 512]
                            ).then_inc(sem(f"p4st{s_cls}"), 16)
                inc(f"p4st{s_cls}", 16)

        for tb in range(NTB):
            if tb < NTB - 1:
                w2q_sloads([((tb * 2) * NJC + jc, jc, 0) for jc in range(NJC)])
                stores_co(tb, 0)
                w2q_sloads([((tb * 2 + 1) * NJC + jc, jc, 1) for jc in range(NJC)])
                stores_co(tb, 1)
            else:
                w2q_sloads([(tb * 2 * NJC + 2 * jc + co, jc, co)
                            for jc in range(NJC) for co in range(2)])
                stores_co(tb, 0)
                stores_co(tb, 1)
        for s_cls in range(4):
            q.wait_ge(sem(f"p4st{s_cls}"), val(f"p4st{s_cls}"))
    on("sync", p4_sync)

    g_yat.__exit__(None, None, None)
    st.close()
    return nc


def colswizzle_test(w, nb, bw):
    wt = np.asarray(w, dtype=np.float32).T
    return np.ascontiguousarray(
        wt.reshape(8, 128, nb, bw).transpose(2, 1, 0, 3).reshape(nb, 128, 8 * bw))


_NC_CACHE = [None]


def kernel(L0_attn_x, L0_attn_wq_w, L0_attn_wq_b, L0_attn_wk_w, L0_attn_wk_b,
           L0_attn_wv_w, L0_attn_wv_b, L0_attn_wo_w, L0_attn_wo_b,
           L0_ffn_w1, L0_ffn_conv_w, L0_ffn_conv_b, L0_ffn_w2):
    f32 = lambda a: np.ascontiguousarray(np.asarray(a, dtype=np.float32))
    x = f32(L0_attn_x)

    def colswizzle(w, nb, bw):
        wt = np.asarray(w, dtype=np.float32).T
        return np.ascontiguousarray(
            wt.reshape(8, 128, nb, bw).transpose(2, 1, 0, 3).reshape(nb, 128, 8 * bw))

    wq_t = colswizzle(L0_attn_wq_w, 8, 128)
    wk_t = colswizzle(L0_attn_wk_w, 8, 128)
    wv_t = colswizzle(L0_attn_wv_w, 4, 256)
    wo_t = f32(np.asarray(L0_attn_wo_w).T)
    w1_t = f32(np.asarray(L0_ffn_w1).T)
    w2_t = f32(np.asarray(L0_ffn_w2).T)
    cw = f32(np.asarray(L0_ffn_conv_w).reshape(32, 128, 3).transpose(1, 0, 2).reshape(128, 96))
    cb = f32(np.asarray(L0_ffn_conv_b).reshape(32, 128).T)
    bq = f32(np.asarray(L0_attn_wq_b).reshape(8, 128).T)
    bk = f32(np.asarray(L0_attn_wk_b).reshape(8, 128).T)
    bv = f32(np.asarray(L0_attn_wv_b).reshape(1, C))
    bo = f32(np.asarray(L0_attn_wo_b).reshape(1, C))
    identf = np.eye(128, dtype=np.float32)
    identb = np.eye(128, dtype=ml_dtypes.bfloat16)
    onesr = np.ones((1, 128), dtype=np.float32)
    if _NC_CACHE[0] is None:
        _NC_CACHE[0] = _build()
    nc = _NC_CACHE[0]

    shared = dict(wq_t=wq_t, wk_t=wk_t, wv_t=wv_t, wo_t=wo_t, w1_t=w1_t,
                  w2_t=w2_t, cw=cw, cb=cb, bq=bq, bk=bk, bv=bv, bo=bo,
                  identf=identf, identb=identb, onesr=onesr)
    in_maps = [dict(shared, x=np.ascontiguousarray(x[b])) for b in range(8)]
    res = run_bass_kernel_spmd(nc, in_maps, core_ids=list(range(8)))
    out = np.stack([np.asarray(res.results[b]["out"], dtype=np.float32)
                    for b in range(8)], axis=0)
    return out


# version 25
# speedup vs baseline: 1.1159x; 1.0969x over previous
import numpy as np
import ml_dtypes
from contextlib import ExitStack

import concourse.bass as bass
import concourse.mybir as mybir
from concourse.bass_utils import run_bass_kernel_spmd
from concourse.alu_op_type import AluOpType as OP

F32 = mybir.dt.float32
BF16 = mybir.dt.bfloat16
AF = mybir.ActivationFunctionType
AX = mybir.AxisListType.X

T, C, H, DH, DF = 2048, 1024, 16, 64, 4096
NT = T // 128
NCH = C // 128
NJC = DF // 128
NHP = H // 2
QB = 512
NQB = T // QB
TBS = 512
NTB = T // TBS
M32 = 12582912.0
M64 = 6755399441055744.0
EPS = 1e-5
B0 = 20.0
NMASK = -1.0e9
W1CH = 32
W2CH = 32
AMX_OFF = (640, 768, 1664, 1792)


def _build():
    nc = bass.Bass("TRN2", target_bir_lowering=False, debug=False)

    def din(name, shape, dt=F32):
        return nc.dram_tensor(name, list(shape), dt, kind="ExternalInput").ap()

    x_d = din("x", (T, C))
    wq_d = din("wq_t", (8, 128, 1024)); wk_d = din("wk_t", (8, 128, 1024))
    wv_d = din("wv_t", (4, 128, 2048)); wo_d = din("wo_t", (C, C))
    w1_d = din("w1_t", (C, DF)); w2_d = din("w2_t", (DF, C))
    cw_d = din("cw", (128, 96)); cb_d = din("cb", (128, 32))
    bq_d = din("bq", (128, 8)); bk_d = din("bk", (128, 8))
    bv_d = din("bv", (1, C)); bo_d = din("bo", (1, C))
    idf_d = din("identf", (128, 128))
    idb_d = din("identb", (128, 128), BF16)
    ones_d = din("onesr", (1, 128))

    out_d = nc.dram_tensor("out", [T, C], F32, kind="ExternalOutput").ap()
    w1q_d = nc.dram_tensor("w1q_stage", [NJC, 128, 1024], BF16, kind="Internal").ap()
    w2q_d = nc.dram_tensor("w2q_stage", [DF, C], BF16, kind="Internal").ap()

    st = ExitStack()
    E = st.enter_context

    semn, cnt = {}, {}
    def sem(name):
        if name not in semn:
            semn[name] = E(nc.semaphore(name))
            cnt[name] = 0
        return semn[name]
    def inc(name, by=1):
        sem(name); cnt[name] += by
        return cnt[name]
    def val(name):
        sem(name); return cnt[name]

    identf = E(nc.sbuf_tensor("identf_s", [128, 128], F32))
    onesr = E(nc.sbuf_tensor("onesr_s", [1, 128], F32))
    onesc = E(nc.sbuf_tensor("onesc", [128, 1], F32))
    bqc = E(nc.sbuf_tensor("bqc", [128, 8], F32))
    bkc = E(nc.sbuf_tensor("bkc", [128, 8], F32))
    cwb = E(nc.sbuf_tensor("cwb", [128, 96], F32))
    cws = E(nc.sbuf_tensor("cws", [128, 96], F32))
    cbc = E(nc.sbuf_tensor("cbc", [128, 32], F32))
    meff = E(nc.sbuf_tensor("meff", [128, 1], F32))
    meffneg = E(nc.sbuf_tensor("meffneg", [128, 1], F32))
    mask = E(nc.sbuf_tensor("mask", [128, 16], F32))
    bias_exp = E(nc.sbuf_tensor("bias_exp", [128, 16], F32))
    invsx = E(nc.sbuf_tensor("invsx", [128, 16], F32))
    shc = E(nc.sbuf_tensor("shc", [128, 16], F32))
    sh_cols = E(nc.sbuf_tensor("sh_cols", [128, 16], F32))
    wsum1 = E(nc.sbuf_tensor("wsum1", [128, W1CH], F32))
    wsum2 = E(nc.sbuf_tensor("wsum2", [128, W2CH], F32))
    sw1c = E(nc.sbuf_tensor("sw1c", [128, 1], F32))
    sw2c = E(nc.sbuf_tensor("sw2c", [128, 1], F32))
    mw2c = E(nc.sbuf_tensor("mw2c", [128, 1], F32))
    tiny = E(nc.sbuf_tensor("tiny", [128, 16], F32))
    trow = E(nc.sbuf_tensor("trow", [1, 512], F32))
    xqT = E(nc.sbuf_tensor("xqT", [128, NCH * 2050], BF16))
    xqTv = xqT.ap().rearrange("p (c t) -> p c t", c=NCH)
    g_bor = nc.sbuf_tensor("bor", [1, 1024], F32); bor = g_bor.__enter__()
    g_onesb = nc.sbuf_tensor("onesb", [1, 128], BF16); onesb = g_onesb.__enter__()
    g_bvb = nc.sbuf_tensor("bvb", [1, 1024], BF16); bvb = g_bvb.__enter__()
    g_ctx = nc.sbuf_tensor("ctx", [128, NCH * 2048], BF16); ctx = g_ctx.__enter__()
    ctxv = ctx.ap().rearrange("p (c t) -> p c t", c=NCH)
    g_dr = nc.sbuf_tensor("denrow", [1, 1024], F32); denrow = g_dr.__enter__()
    g_rr = nc.sbuf_tensor("rtile", [128, 512], F32); rtile = g_rr.__enter__()

    def w1_src(ch):
        r, h = ch // 4, ch % 4
        return w1_d[r * 128:(r + 1) * 128, h * 1024:(h + 1) * 1024]

    def w2_src(ch):
        return w2_d[ch * 128:(ch + 1) * 128, :]

    psA = E(nc.psum_tensor("psA", [128, 2048], F32))
    psC = E(nc.psum_tensor("psC", [128, 1536], F32))
    g_psB = nc.psum_tensor("psB", [128, 1024], BF16)
    psB = g_psB.__enter__()

    blk = E(nc.Block())
    def on(engname, fn):
        getattr(blk, engname)(fn)

    def p0_sync(q):
        for dst, src in [
            (identf[:], idf_d[:]), (onesr[:], ones_d[:]),
            (bqc[:], bq_d[:]), (bkc[:], bk_d[:]),
            (bor[:], bo_d[:]),
            (cwb[:], cw_d[:]), (cbc[:], cb_d[:]),
        ]:
            q.dma_start(out=dst, in_=src).then_inc(sem("cst"), 16)
            inc("cst", 16)
    on("sync", p0_sync)
    CST = val("cst")

    def p0_vec(v):
        v.wait_ge(sem("cst"), CST)
        v.memset(onesb[:], 1.0).then_inc(sem("p0v"), 1); inc("p0v")
        v.memset(onesc[:], 1.0).then_inc(sem("p0v"), 1); inc("p0v")
        v.wait_ge(sem("p0v"), val("p0v"))
        v.tensor_scalar(tiny[:, 0:1], onesc[:], 2.0 ** -30, 1.0,
                        OP.add, OP.subtract).then_inc(sem("p0v"), 1); inc("p0v")
        v.wait_ge(sem("p0v"), val("p0v"))
        v.tensor_scalar(tiny[:, 1:2], tiny[:, 0:1], 2.0 ** 30, None,
                        OP.mult).then_inc(sem("p0v"), 1); inc("p0v")
        v.wait_ge(sem("p0v"), val("p0v"))
        v.tensor_scalar(meff[:], tiny[:, 1:2], M64, M32,
                        OP.mult, OP.add).then_inc(sem("p0v"), 1); inc("p0v")
        v.wait_ge(sem("p0v"), val("p0v"))
        v.tensor_scalar(meffneg[:], meff[:], -1.0, None,
                        OP.mult).then_inc(sem("p0v"), 1); inc("p0v")
    on("vector", p0_vec)
    MEFF = val("p0v")

    g_xnT = nc.sbuf_tensor("xnT", [128, NCH * 2048], BF16)
    xnT = g_xnT.__enter__()
    xnTv = xnT.ap().rearrange("p (c t) -> p c t", c=NCH)
    g_wf = nc.sbuf_tensor("wf", [128, 4 * 1024], F32)
    wf = g_wf.__enter__()
    g_x_t = nc.sbuf_tensor("x_t", [128, 2 * 1024], F32); x_t = g_x_t.__enter__()
    g_xn_t = nc.sbuf_tensor("xn_t", [128, 2 * 1024], F32); xn_t = g_xn_t.__enter__()
    g_xnb = nc.sbuf_tensor("xnb_t", [128, 2 * 1024], BF16); xnb_t = g_xnb.__enter__()
    g_xqb = nc.sbuf_tensor("xqb_t", [128, 2 * 1024], BF16); xqb_t = g_xqb.__enter__()
    g_stt = nc.sbuf_tensor("stt", [128, 2 * 16], F32); stt = g_stt.__enter__()
    g_identb = nc.sbuf_tensor("identb_s", [128, 128], BF16)
    identb = g_identb.__enter__()

    def p1_sync(q):
        q.dma_start(out=identb[:], in_=idb_d[:]).then_inc(sem("cstb"), 16)
        inc("cstb", 16)
        for i in range(NT):
            b = i % 2
            if i >= 2:
                q.wait_ge(sem(f"xf{b}"), i // 2)
            q.dma_start(out=x_t[:, b * 1024:(b + 1) * 1024],
                        in_=x_d[i * 128:(i + 1) * 128, :]).then_inc(sem(f"xl{b}"), 16)
            inc(f"xl{b}", 16)
    on("sync", p1_sync)

    def p1_tensor(t):
        t.wait_ge(sem("cst"), CST)
        t.wait_ge(sem("cstb"), 16)
        for i in range(NT):
            b = i % 2
            for g, semname in enumerate(["p1xb", "p1xq"]):
                srcbuf = xnb_t if g == 0 else xqb_t
                t.wait_ge(sem(semname), i + 1)
                if 2 * i + g >= 1:
                    t.wait_ge(sem("p1ev"), 2 * i + g)
                for j in range(8):
                    mm = t.transpose(psB[:, j * 128:(j + 1) * 128],
                                     srcbuf[:, b * 1024 + j * 128:b * 1024 + (j + 1) * 128],
                                     identb[:])
                mm.then_inc(sem("p1tpm"), 1)
                inc("p1tpm")
    on("tensor", p1_tensor)

    def p1_scalar(s):
        NW = W1CH + W2CH

        def wdma(ch):
            if ch >= 2:
                s.wait_ge(sem("wabs"), ch - 1)
            src = w1_src(ch) if ch < W1CH else w2_src(ch - W1CH)
            s.dma_start(out=wf[:, (ch % 3) * 1024:(ch % 3 + 1) * 1024],
                        in_=src).then_inc(sem(f"wl{ch % 3}"), 16)
            inc(f"wl{ch % 3}", 16)

        def wabs(ch):
            acc = wsum1[:, ch:ch + 1] if ch < W1CH else wsum2[:, ch - W1CH:ch - W1CH + 1]
            s.wait_ge(sem(f"wl{ch % 3}"), 16 * (ch // 3 + 1))
            s.activation(out=wf[:, (ch % 3) * 1024:(ch % 3 + 1) * 1024],
                         in_=wf[:, (ch % 3) * 1024:(ch % 3 + 1) * 1024],
                         func=AF.Abs, accum_out=acc).then_inc(sem("wabs"), 1)
            inc("wabs")

        def quant_tail(i2):
            b2 = i2 % 2
            xnt2 = xn_t[:, b2 * 1024:(b2 + 1) * 1024]
            s.wait_ge(sem("p1q1"), i2 + 1)
            s.activation(out=xnt2, in_=xnt2, func=AF.Identity,
                         bias=meffneg[:, 0:1]).then_inc(sem("p1rt"), 1)
            inc("p1rt")
            s.wait_ge(sem("p1rt"), val("p1rt"))
            if i2 >= 2:
                s.wait_ge(sem("p1tpm"), 2 * i2 - 2)
            s.activation(out=xqb_t[:, b2 * 1024:(b2 + 1) * 1024], in_=xnt2,
                         func=AF.Identity,
                         scale=invsx[:, i2:i2 + 1]).then_inc(sem("p1xq"), 1)
            inc("p1xq")

        wdma(0); wdma(1)
        s.wait_ge(sem("p0v"), MEFF)
        for i in range(NT):
            b = i % 2
            s.wait_ge(sem(f"xl{b}"), 16 * (i // 2 + 1))
            if i >= 1:
                s.wait_ge(sem("p1sq"), i)
            s.activation(out=psA[:, 0:1024], in_=x_t[:, b * 1024:(b + 1) * 1024],
                         func=AF.Square,
                         accum_out=stt[:, b * 16 + 0:b * 16 + 1]).then_inc(sem("p1sq"), 1)
            inc("p1sq")
            for ch in (4 * i, 4 * i + 1):
                wabs(ch)
                if ch + 2 < NW:
                    wdma(ch + 2)
            if i >= 1:
                quant_tail(i - 1)
            s.wait_ge(sem("p1v"), 16 * i + 5)
            s.activation(out=stt[:, b * 16 + 3:b * 16 + 4],
                         in_=stt[:, b * 16 + 2:b * 16 + 3],
                         func=AF.Sqrt).then_inc(sem("p1sqrt"), 1)
            inc("p1sqrt")
            for ch in (4 * i + 2, 4 * i + 3):
                wabs(ch)
                if ch + 2 < NW:
                    wdma(ch + 2)
        quant_tail(NT - 1)
    on("scalar", p1_scalar)

    def p1_vector(v):
        v.memset(xqTv[:, :, 0:1], 0.0).then_inc(sem("p1z"), 1); inc("p1z")
        v.memset(xqTv[:, :, 2049:2050], 0.0).then_inc(sem("p1z"), 1); inc("p1z")

        def evac(i):
            for g in range(2):
                v.wait_ge(sem("p1tpm"), 2 * i + g + 1)
                if g == 0:
                    dst = xnTv[:, :, i * 128:(i + 1) * 128]
                else:
                    dst = xqTv[:, :, i * 128 + 1:(i + 1) * 128 + 1]
                v.tensor_copy(dst, psB.ap().rearrange("p (c t) -> p c t", c=8)
                              ).then_inc(sem("p1ev"), 1)
                inc("p1ev")

        for i in range(NT):
            b = i % 2
            c = lambda j: stt[:, b * 16 + j:b * 16 + j + 1]
            xt = x_t[:, b * 1024:(b + 1) * 1024]
            xnt = xn_t[:, b * 1024:(b + 1) * 1024]
            v.wait_ge(sem(f"xl{b}"), 16 * (i // 2 + 1))
            v.reduce_sum(c(1), xt, axis=AX).then_inc(sem("p1v"), 1); inc("p1v")
            v.wait_ge(sem("p1v"), val("p1v"))
            v.tensor_scalar(c(1), c(1), -1.0 / C, None, OP.mult).then_inc(sem("p1v"), 1); inc("p1v")
            v.wait_ge(sem("p1v"), val("p1v"))
            v.tensor_tensor(c(4), c(1), c(1), OP.mult).then_inc(sem("p1v"), 1); inc("p1v")
            v.wait_ge(sem("p1sq"), i + 1)
            v.wait_ge(sem("p1v"), val("p1v"))
            v.scalar_tensor_tensor(c(2), c(0), 1.0 / C, c(4),
                                   OP.mult, OP.subtract).then_inc(sem("p1v"), 1); inc("p1v")
            v.wait_ge(sem("p1v"), val("p1v"))
            v.tensor_scalar(c(2), c(2), EPS, None, OP.add).then_inc(sem("p1v"), 1); inc("p1v")
            v.wait_ge(sem("p1sqrt"), i + 1)
            v.reciprocal(c(5), c(3)).then_inc(sem("p1v"), 1); inc("p1v")
            v.wait_ge(sem("p1v"), val("p1v"))
            v.tensor_tensor(c(6), c(5), c(5), OP.mult).then_inc(sem("p1v"), 1); inc("p1v")
            v.wait_ge(sem("p1v"), val("p1v"))
            v.tensor_tensor(c(6), c(6), c(2), OP.mult).then_inc(sem("p1v"), 1); inc("p1v")
            v.wait_ge(sem("p1v"), val("p1v"))
            v.tensor_scalar(c(6), c(6), -0.5, 1.5, OP.mult, OP.add).then_inc(sem("p1v"), 1); inc("p1v")
            v.wait_ge(sem("p1v"), val("p1v"))
            v.tensor_tensor(c(7), c(6), c(5), OP.mult).then_inc(sem("p1v"), 1); inc("p1v")
            v.wait_ge(sem("p1v"), val("p1v"))
            if i >= 2:
                v.wait_ge(sem("p1xq"), i - 1)
            v.tensor_scalar(xnt, xt, c(1), c(7), OP.add, OP.mult).then_inc(
                sem(f"xf{b}"), 1)
            inc(f"xf{b}")
            v.wait_ge(sem(f"xf{b}"), val(f"xf{b}"))
            v.tensor_scalar(mask[:, i:i + 1], xnt[:, 0:1], 0.0, NMASK,
                            OP.is_equal, OP.mult).then_inc(sem("p1v"), 1); inc("p1v")
            if i >= 2:
                v.wait_ge(sem("p1ev"), 2 * (i - 1))
            v.tensor_copy(xnb_t[:, b * 1024:(b + 1) * 1024], xnt).then_inc(sem("p1xb"), 1); inc("p1xb")
            v.wait_ge(sem("p1v"), val("p1v"))
            v.reduce_max(c(8), xnt, axis=AX,
                         apply_absolute_value=True).then_inc(sem("p1v"), 1); inc("p1v")
            v.wait_ge(sem("p1v"), val("p1v"))
            v.tensor_scalar(c(9), c(8), 1e-5, None, OP.max).then_inc(sem("p1v"), 1); inc("p1v")
            v.wait_ge(sem("p1v"), val("p1v"))
            v.tensor_scalar(invsx[:, i:i + 1], c(9), 1.0 / 127.0, None,
                            OP.mult).then_inc(sem("p1v"), 1); inc("p1v")
            v.reciprocal(c(10), c(9)).then_inc(sem("p1v"), 1); inc("p1v")
            v.wait_ge(sem("p1v"), val("p1v"))
            v.tensor_scalar(c(11), c(10), 127.0, None, OP.mult).then_inc(sem("p1v"), 1); inc("p1v")
            v.wait_ge(sem("p1v"), val("p1v"))
            v.wait_ge(sem("p0v"), MEFF)
            v.tensor_scalar(xnt, xnt, c(11), meff[:, 0:1],
                            OP.mult, OP.add).then_inc(sem("p1q1"), 1); inc("p1q1")
            if i >= 1:
                evac(i - 1)
        evac(NT - 1)
        v.wait_ge(sem("p1v"), val("p1v"))
        v.tensor_scalar(bias_exp[:], mask[:], -B0, None, OP.add).then_inc(sem("p1bias"), 1); inc("p1bias")
    on("vector", p1_vector)
    P1EV = val("p1ev")
    P1BIAS = val("p1bias")
    for _g in (g_identb, g_stt, g_xqb, g_xnb, g_xn_t, g_x_t):
        _g.__exit__(None, None, None)
    g_psB.__exit__(None, None, None)
    psD = E(nc.psum_tensor("psD", [128, 512], F32))



    WABS = W1CH + W2CH

    def wmean_vec(v):
        v.wait_ge(sem("wabs"), WABS)
        v.reduce_sum(tiny[:, 2:3], wsum1[:], axis=AX).then_inc(sem("wmv"), 1); inc("wmv")
        v.reduce_sum(tiny[:, 3:4], wsum2[:], axis=AX).then_inc(sem("wmv"), 1); inc("wmv")
    on("vector", wmean_vec)
    WMV1 = val("wmv")

    def wmean_tensor(t):
        t.wait_ge(sem("p0v"), MEFF)
        t.wait_ge(sem("wmv"), WMV1)
        t.matmul(psC[0:1, 0:1], tiny[:, 2:3], onesc[:], start=True, stop=True)
        mm = t.matmul(psC[0:1, 1:2], tiny[:, 3:4], onesc[:], start=True, stop=True)
        mm.then_inc(sem("wmt"), 1); inc("wmt")
    on("tensor", wmean_tensor)
    WMT1 = val("wmt")

    def wmean_vec2(v):
        v.wait_ge(sem("wmt"), WMT1)
        v.tensor_scalar(tiny[0:1, 4:6], psC[0:1, 0:2], 1.0 / (C * DF), 1e-5,
                        OP.mult, OP.max).then_inc(sem("wmv"), 1); inc("wmv")
        v.wait_ge(sem("wmv"), val("wmv"))
        v.reciprocal(tiny[0:1, 6:8], tiny[0:1, 4:6]).then_inc(sem("wmv"), 1); inc("wmv")
    on("vector", wmean_vec2)
    WMV2 = val("wmv")

    def wmean_tensor2(t):
        t.wait_ge(sem("wmv"), WMV2)
        t.matmul(psC[:, 8:9], onesr[0:1, 0:128], tiny[0:1, 6:7], start=True, stop=True)
        t.matmul(psC[:, 9:10], onesr[0:1, 0:128], tiny[0:1, 7:8], start=True, stop=True)
        mm = t.matmul(psC[:, 10:11], onesr[0:1, 0:128], tiny[0:1, 5:6],
                      start=True, stop=True)
        mm.then_inc(sem("wmt"), 1); inc("wmt")
    on("tensor", wmean_tensor2)
    WMT2 = val("wmt")

    def wmean_vec3(v):
        v.wait_ge(sem("wmt"), WMT2)
        v.tensor_copy(sw1c[:], psC[:, 8:9]).then_inc(sem("wmv"), 1); inc("wmv")
        v.tensor_copy(sw2c[:], psC[:, 9:10]).then_inc(sem("wmv"), 1); inc("wmv")
        v.tensor_copy(mw2c[:], psC[:, 10:11]).then_inc(sem("wmv"), 1); inc("wmv")
        v.wait_ge(sem("wmv"), val("wmv"))
        v.reciprocal(tiny[:, 8:9], sw1c[:]).then_inc(sem("wmv"), 1); inc("wmv")
        v.wait_ge(sem("wmv"), val("wmv"))
        v.tensor_scalar(cws[:], cwb[:], tiny[:, 8:9], None,
                        OP.mult).then_inc(sem("wmv"), 1); inc("wmv")
    on("vector", wmean_vec3)
    WSC = val("wmv")

    g_wqb = nc.sbuf_tensor("wqb", [128, 2 * 1024], BF16)
    wqb = g_wqb.__enter__()

    NW = W1CH + W2CH

    def qdma2(ch, eng):
        src = w1_src(ch) if ch < W1CH else w2_src(ch - W1CH)
        eng.dma_start(out=wf[:, (ch % 4) * 1024:(ch % 4 + 1) * 1024],
                      in_=src).then_inc(sem(f"ql{ch % 4}"), 16)
        inc(f"ql{ch % 4}", 16)

    def wq_act1(s, ch):
        scl = sw1c if ch < W1CH else sw2c
        s.wait_ge(sem(f"ql{ch % 4}"), 16 * (ch // 4 + 1))
        s.activation(out=wf[:, (ch % 4) * 1024:(ch % 4 + 1) * 1024],
                     in_=wf[:, (ch % 4) * 1024:(ch % 4 + 1) * 1024],
                     func=AF.Identity,
                     bias=meff[:, 0:1], scale=scl[:, 0:1]).then_inc(sem("wqs"), 1)
        inc("wqs")

    def wq_act2(s, ch):
        s.wait_ge(sem("wqs"), 2 * ch + 1)
        s.activation(out=wf[:, (ch % 4) * 1024:(ch % 4 + 1) * 1024],
                     in_=wf[:, (ch % 4) * 1024:(ch % 4 + 1) * 1024],
                     func=AF.Identity,
                     bias=meffneg[:, 0:1]).then_inc(sem("wqs"), 1)
        inc("wqs")

    def wq_store(q):
        for ch in range(W1CH + W2CH):
            b = ch % 2
            q.wait_ge(sem("wqv"), ch + 1)
            if ch < W1CH:
                r, h = ch // 4, ch % 4
                dst = w1q_d[h * 8:(h + 1) * 8, :, r * 128:(r + 1) * 128].rearrange(
                    "j p m -> p j m")
                src = wqb[:, b * 1024:(b + 1) * 1024].rearrange("p (j m) -> p j m", j=8)
            else:
                dst = w2q_d[(ch - W1CH) * 128:(ch - W1CH + 1) * 128, :]
                src = wqb[:, b * 1024:(b + 1) * 1024]
            q.dma_start(out=dst, in_=src).then_inc(sem(f"wqst{b}"), 16)
            inc(f"wqst{b}", 16)
    on("sync", wq_store)
    WQST0 = val("wqst0")
    WQST1 = val("wqst1")

    w1q_sb = E(nc.sbuf_tensor("w1q_sb", [128, 2 * 1024], BF16, side="right"))
    w2q_sb = E(nc.sbuf_tensor("w2q_sb", [128, 4 * 512], BF16, side="right"))
    g_qT = nc.sbuf_tensor("qroll", [128, 2 * 2048], BF16, side="right"); qT = g_qT.__enter__()
    g_kT = nc.sbuf_tensor("kroll", [128, 2 * 2048], BF16, side="right"); kT = g_kT.__enter__()
    qTv = qT.ap().rearrange("p (s t) -> p s t", s=2)
    kTv = kT.ap().rearrange("p (s t) -> p s t", s=2)
    g_v_sb = nc.sbuf_tensor("vroll", [128, 2 * NT * 260], BF16, side="right"); v_sb = g_v_sb.__enter__()
    v_v = v_sb.ap().rearrange("p (s t h d) -> p s t h d", s=2, t=NT, h=4)
    g_probs = nc.sbuf_tensor("probs", [128, 4 * 1024], BF16, side="right"); probs = g_probs.__enter__()
    g_wblk = nc.sbuf_tensor("wblk", [128, 2 * 1024], BF16); wblk = g_wblk.__enter__()
    g_wvq = nc.sbuf_tensor("wvq", [128, 2 * 2048], BF16); wvq = g_wvq.__enter__()

    prods = ([("q", 0, qb) for qb in range(4)] + [("k", 0, qb) for qb in range(4)]
             + [("v", 0, ti) for ti in range(NT)])
    PROLOG = len(prods)
    seg_prod = {}
    for hp in range(NHP):
        lst = []
        if hp < NHP - 1:
            lst += [("q", hp + 1, qb) for qb in range(4)]
            lst += [("k", hp + 1, qb) for qb in range(4)]
        if hp < 6:
            lst += [("v", hp // 2 + 1, (hp % 2) * 8 + m) for m in range(8)]
        seg_prod[hp] = lst
        prods = prods + lst
    pidx = {item: i for i, item in enumerate(prods)}
    seg_sched = {}
    for hp in range(NHP):
        chunks = [(item, half) for item in seg_prod[hp] for half in (0, 1)]
        mp = {}
        for m, ch_ in enumerate(chunks):
            mp.setdefault(m * 56 // max(len(chunks), 1), []).append(ch_)
        seg_sched[hp] = mp
    NQK, NV = {}, {}
    _nq = _nv = 0
    for i, item in enumerate(prods):
        if item[0] == "v":
            _nv += 1
        else:
            _nq += 1
        NQK[i], NV[i] = _nq, _nv
    last_k = {hp: pidx[("k", hp, 3)] for hp in range(NHP)}

    def p23_gload(g):
        g.dma_start(out=bvb[:], in_=bv_d[:]).then_inc(sem("p2bv"), 16)
        inc("p2bv", 16)
        g.wait_ge(sem("p1xq"), NT)
        g.wait_ge(sem("p1tpm"), 2 * NT)
        g.wait_ge(sem("p1sq"), NT)
        g.dma_start(out=wblk[:, 0:1024], in_=wq_d[0]).then_inc(sem("lq"), 16); inc("lq", 16)
        g.dma_start(out=wblk[:, 1024:2048], in_=wk_d[0]).then_inc(sem("lk"), 16); inc("lk", 16)
        g.dma_start(out=wvq[:, 0:2048], in_=wv_d[0]).then_inc(sem("lv0"), 16); inc("lv0", 16)
        g.dma_start(out=wvq[:, 2048:4096], in_=wv_d[1]).then_inc(sem("lv1"), 16); inc("lv1", 16)
        for hp in range(1, NHP):
            g.wait_ge(sem("qkp"), 8 * (hp - 1) + 4)
            g.dma_start(out=wblk[:, 0:1024], in_=wq_d[hp]).then_inc(sem("lq"), 16); inc("lq", 16)
            g.wait_ge(sem("qkp"), 8 * hp)
            g.dma_start(out=wblk[:, 1024:2048], in_=wk_d[hp]).then_inc(sem("lk"), 16); inc("lk", 16)
            if hp == 2:
                g.wait_ge(sem("vmm"), 16)
                g.dma_start(out=wvq[:, 0:2048], in_=wv_d[2]).then_inc(sem("lv0"), 16); inc("lv0", 16)
            if hp == 4:
                g.wait_ge(sem("vmm"), 32)
                g.dma_start(out=wvq[:, 2048:4096], in_=wv_d[3]).then_inc(sem("lv1"), 16); inc("lv1", 16)
    on("gpsimd", p23_gload)

    def p23_tensor(t):
        t.wait_ge(sem("p0v"), MEFF)
        t.wait_ge(sem("p1ev"), P1EV)
        t.wait_ge(sem("p1bias"), P1BIAS)

        def prod_block(item, half=None):
            kind, a, b = item
            i = pidx[item]
            ccs = range(NCH) if half is None else range(half * 4, half * 4 + 4)
            if i >= 1 and half != 1:
                t.wait_ge(sem("pde"), i)
            if kind in ("q", "k"):
                if half != 1:
                    t.wait_ge(sem("lq" if kind == "q" else "lk"), 16 * (a + 1))
                off = 0 if kind == "q" else 1024
                for cc in ccs:
                    mm = t.matmul(psD[:, 0:512],
                                  wblk[:, off + cc * 128:off + (cc + 1) * 128],
                                  xnTv[:, cc, b * 512:(b + 1) * 512],
                                  start=(cc == 0), stop=(cc == NCH - 1))
                if half != 0:
                    mm.then_inc(sem("qkp"), 1)
                    inc("qkp")
            else:
                qv, ti = a, b
                if half != 1:
                    t.wait_ge(sem(f"lv{qv % 2}"), 16 * (qv // 2 + 1))
                for cc in ccs:
                    t.matmul(psD[:, 0:256],
                             xnTv[:, cc, ti * 128:(ti + 1) * 128],
                             wvq[:, (qv % 2) * 2048 + cc * 256:(qv % 2) * 2048 + (cc + 1) * 256],
                             start=(cc == 0), stop=False)
                if half != 0:
                    if qv == 0 and ti == 0:
                        t.wait_ge(sem("p2bv"), 16)
                    mm = t.matmul(psD[:, 0:256], onesb[0:1, 0:128],
                                  bvb[0:1, qv * 256:(qv + 1) * 256], start=False, stop=True)
                    mm.then_inc(sem("vmm"), 1)
                    inc("vmm")

        def emit_ctx(jj):
            itj, kcj = divmod(jj, NT)
            hpj = itj // NQB
            qvj = hpj // 2
            laj = 2 * (hpj % 2)
            if jj == 0:
                t.wait_ge(sem("vone"), 2)
            if kcj == 0:
                t.wait_ge(sem("p3ev2"), 3 * itj)
            t.wait_ge(sem("p3exp"), jj + 1)
            t.wait_ge(sem("pde"), pidx[("v", qvj, kcj)] + 1)
            pslot = jj % 4
            t.matmul(psC[0:65, 0:512],
                     v_v[:, qvj % 2, kcj, laj, 0:65],
                     probs[:, pslot * 1024:pslot * 1024 + 512],
                     start=(kcj == 0), stop=(kcj == NT - 1))
            mm = t.matmul(psC[0:65, 512:1024],
                          v_v[:, qvj % 2, kcj, laj + 1, 0:65],
                          probs[:, pslot * 1024 + 512:pslot * 1024 + 1024],
                          start=(kcj == 0), stop=(kcj == NT - 1))
            mm.then_inc(sem("p3pf"), 1)
            inc("p3pf")
        def emit_den(itj):
            t.wait_ge(sem("p3ev2"), 3 * itj + 3)
            if itj > 0:
                t.wait_ge(sem("p3nv"), itj)
            t.matmul(psC[0:64, 1024:1536], onesr[0:1, 0:64], denrow[0:1, 0:512],
                     start=True, stop=True)
            mm = t.matmul(psC[64:128, 1024:1536], onesr[0:1, 0:64],
                          denrow[0:1, 512:1024], start=True, stop=True)
            mm.then_inc(sem("p3nb"), 1)
            inc("p3nb")

        for item in prods[:PROLOG]:
            prod_block(item)
        NJ3 = NHP * NQB * NT
        for j in range(NJ3):
            it, kc = divmod(j, NT)
            hp, qb = divmod(it, NQB)
            jl = j - hp * 64
            if jl == 0:
                t.wait_ge(sem("pde"), last_k[hp] + 1)
            pp = kc % 2
            sA = psA[:, pp * 1024:pp * 1024 + 512]
            sB = psA[:, pp * 1024 + 512:pp * 1024 + 1024]
            if j >= 2:
                t.wait_ge(sem("p3exp"), j - 1)
            t.matmul(sA, kTv[0:64, hp % 2, kc * 128:(kc + 1) * 128],
                     qTv[0:64, hp % 2, qb * 512:(qb + 1) * 512],
                     start=True, stop=True, tile_position=(0, 0))
            mm = t.matmul(sB, kTv[64:128, hp % 2, kc * 128:(kc + 1) * 128],
                          qTv[64:128, hp % 2, qb * 512:(qb + 1) * 512],
                          start=True, stop=True, tile_position=(64, 0))
            mm.then_inc(sem("p3s"), 1)
            inc("p3s")
            for item, half in seg_sched[hp].get(jl, []):
                prod_block(item, half)
            if j >= 1:
                emit_ctx(j - 1)
            if j >= 3 and (j - 3) % NT == NT - 1:
                emit_den((j - 3) // NT)
        emit_ctx(NJ3 - 1)
        emit_den(NHP * NQB - 1)
    on("tensor", p23_tensor)
    P3PF = val("p3pf")

    def p23_scalar(s):
        s.wait_ge(sem("wmv"), WSC)
        s.wait_ge(sem("p1bias"), P1BIAS)
        for ch in range(4):
            qdma2(ch, s)
        NJ3 = NHP * NQB * NT
        for j in range(NJ3):
            if j % 8 == 4:
                chl = j // 8 + 3
                if 4 <= chl < NW:
                    s.wait_ge(sem("wqv"), chl - 3)
                    qdma2(chl, s)
            pp = j % 2
            pslot = j % 4
            kcn = j % NT
            s.wait_ge(sem("p3s"), j + 1)
            if j >= 4:
                s.wait_ge(sem("p3pf"), j - 3)
            s.activation(out=probs[:, pslot * 1024:(pslot + 1) * 1024],
                         in_=psA[:, pp * 1024:(pp + 1) * 1024],
                         func=AF.Exp, scale=0.125,
                         bias=bias_exp[:, kcn:kcn + 1]).then_inc(sem("p3exp"), 1)
            inc("p3exp")
            if j % NT == NT - 1:
                itb = j // NT
                hpb, qbb = divmod(itb, NQB)
                s.wait_ge(sem("p3pf"), (itb + 1) * NT)
                s.activation(out=ctxv[0:64, hpb, qbb * 512:(qbb + 1) * 512],
                             in_=psC[0:64, 0:512],
                             func=AF.Identity).then_inc(sem("p3ev2"), 1)
                inc("p3ev2")
            if j % 8 == 2 and j // 8 < NW:
                wq_act1(s, j // 8)
            if j % 8 == 5 and j // 8 < NW:
                wq_act2(s, j // 8)
    on("scalar", p23_scalar)

    def p23_vec(v):
        v.memset(v_v[:, 0, :, :, 64:65], 1.0).then_inc(sem("vone"), 1); inc("vone")
        v.memset(v_v[:, 1, :, :, 64:65], 1.0).then_inc(sem("vone"), 1); inc("vone")

        def wclip(ch):
            b4 = ch % 4
            b = ch % 2
            v.wait_ge(sem("wqs"), 2 * ch + 2)
            if ch >= 2:
                v.wait_ge(sem(f"wqst{ch % 2}"), 16 * (ch // 2))
            v.tensor_scalar(wqb[:, b * 1024:(b + 1) * 1024],
                            wf[:, b4 * 1024:(b4 + 1) * 1024],
                            -1.0, 1.0, OP.max, OP.min).then_inc(sem("wqv"), 1); inc("wqv")

        def evac_prod(item):
            kind, a, b = item
            i = pidx[item]
            if kind == "v":
                v.wait_ge(sem("vmm"), NV[i])
                v.tensor_copy(v_v[:, a % 2, b, 0:4, 0:64],
                              psD[:, 0:256].rearrange("p (h d) -> p h d", h=4)
                              ).then_inc(sem("pde"), 1)
            else:
                v.wait_ge(sem("qkp"), NQK[i])
                dstv, bc = (qTv, bqc) if kind == "q" else (kTv, bkc)
                v.tensor_scalar(dstv[:, a % 2, b * 512:(b + 1) * 512],
                                psD[:, 0:512],
                                bc[:, a:a + 1], None, OP.add).then_inc(sem("pde"), 1)
            inc("pde")

        def ctx_norm(it):
            hp, qb = divmod(it, NQB)
            v.wait_ge(sem("p3nb"), it + 1)
            v.reciprocal(rtile[:], psC[:, 1024:1536]).then_inc(sem("p3nv"), 1); inc("p3nv")
            v.wait_ge(sem("p3nv"), val("p3nv"))
            v.tensor_tensor(ctxv[:, hp, qb * 512:(qb + 1) * 512],
                            ctxv[:, hp, qb * 512:(qb + 1) * 512],
                            rtile[:], OP.mult).then_inc(sem("p3tt"), 1); inc("p3tt")

        for item in prods[:PROLOG]:
            evac_prod(item)
        NJ3 = NHP * NQB * NT
        for j in range(NJ3):
            it, kc = divmod(j, NT)
            hp, qb = divmod(it, NQB)
            jl = j - hp * 64
            for item, half in seg_sched[hp].get(jl, []):
                if half == 1:
                    evac_prod(item)
            if j % 8 == 1 and 1 <= j // 8 <= NW:
                wclip(j // 8 - 1)
            if kc == 4 and it >= 1:
                ctx_norm(it - 1)
            if kc == NT - 1:
                v.wait_ge(sem("p3pf"), (it + 1) * NT)
                v.tensor_copy(ctxv[64:128, hp, qb * 512:(qb + 1) * 512],
                              psC[0:64, 512:1024]).then_inc(sem("p3ev2"), 1); inc("p3ev2")
                v.tensor_copy(denrow[0:1, :],
                              psC[64:65, 0:1024]).then_inc(sem("p3ev2"), 1); inc("p3ev2")
        ctx_norm(NHP * NQB - 1)
        wclip(NW - 1)
    on("vector", p23_vec)
    P3TT = val("p3tt")

    for _g in (g_probs, g_v_sb, g_kT, g_qT):
        _g.__exit__(None, None, None)
    for _g in (g_wvq, g_wblk, g_wqb, g_wf, g_xnT):
        _g.__exit__(None, None, None)
    g_wo = nc.sbuf_tensor("wo_bf", [128, 8 * 1024], BF16); wo_bf = g_wo.__enter__()
    g_yat = nc.sbuf_tensor("y_attn", [128, NT * 1024], BF16, side="right"); y_attn = g_yat.__enter__()
    y_av = y_attn.ap().rearrange("p (t m) -> p t m", t=NT)

    def p35_gload(g):
        g.wait_ge(sem("qkp"), 8 * NHP)
        g.wait_ge(sem("vmm"), 4 * NT)
        g.wait_ge(sem("wqv"), NW)
        g.wait_ge(sem("wqst0"), WQST0)
        g.wait_ge(sem("wqst1"), WQST1)
        for cc in range(NCH):
            g.dma_start(out=wo_bf[:, cc * 1024:(cc + 1) * 1024],
                        in_=wo_d[cc * 128:(cc + 1) * 128, :]).then_inc(sem("p3wo"), 16)
            inc("p3wo", 16)
    on("gpsimd", p35_gload)
    P3WO = val("p3wo")

    def p35_tensor(t):
        t.wait_ge(sem("p3tt"), P3TT)
        t.wait_ge(sem("p3wo"), P3WO)
        for ti in range(NT):
            for co in range(2):
                n = ti * 2 + co
                slot = n % 3
                ps = psA[:, slot * 512:(slot + 1) * 512]
                if n >= 3:
                    t.wait_ge(sem("p35oev"), n - 2)
                for cc in range(NCH):
                    t.matmul(ps, ctxv[:, cc, ti * 128:(ti + 1) * 128],
                             wo_bf[:, cc * 1024 + co * 512:cc * 1024 + (co + 1) * 512],
                             start=(cc == 0), stop=False)
                mm = t.matmul(ps, onesr[0:1, 0:128], bor[0:1, co * 512:(co + 1) * 512],
                              start=False, stop=True)
                mm.then_inc(sem("p35o"), 1)
                inc("p35o")
    on("tensor", p35_tensor)

    def p35_vec(v):
        for ti in range(NT):
            for co in range(2):
                n = ti * 2 + co
                slot = n % 3
                v.wait_ge(sem("p35o"), n + 1)
                v.tensor_copy(y_av[:, ti, co * 512:(co + 1) * 512],
                              psA[:, slot * 512:(slot + 1) * 512]).then_inc(sem("p35oev"), 1); inc("p35oev")
    on("vector", p35_vec)
    P35OEV = val("p35oev")

    for _g in (g_wo, g_rr, g_dr, g_ctx, g_bvb, g_onesb, g_bor):
        _g.__exit__(None, None, None)
    h_act = E(nc.sbuf_tensor("h_act", [128, NJC * 512], F32))
    h_v = h_act.ap().rearrange("p (j t) -> p j t", j=NJC)
    hq = E(nc.sbuf_tensor("hq", [128, NJC * 512], BF16))
    hq_v = hq.ap().rearrange("p (j t) -> p j t", j=NJC)
    u_sb = E(nc.sbuf_tensor("u_sb", [128, 3 * 512], F32))
    qscr = E(nc.sbuf_tensor("qscr", [128, 2 * 512], F32))
    qscr2 = E(nc.sbuf_tensor("qscr2", [128, 512], F32))
    shb_sb = E(nc.sbuf_tensor("shb_sb", [128, 512], F32))
    amax1 = E(nc.sbuf_tensor("amax1", [128, 512], F32))
    amax2 = E(nc.sbuf_tensor("amax2", [128, 512], F32))
    xr_t = E(nc.sbuf_tensor("xr_t", [128, 4 * 512], F32))
    out_t = E(nc.sbuf_tensor("out_t", [128, 2 * 1024], F32))

    def w1q_src(jc):
        return w1q_d[jc]

    P35OT = val("p35o")

    def p4_scalar(s):
        s.wait_ge(sem("wqst0"), WQST0)
        s.wait_ge(sem("wqst1"), WQST1)

        def op1_one(tb, jc):
            r = tb * NJC + jc
            pslot = jc % 2
            ub = r % 3
            ps = psA[:, pslot * 1024:pslot * 1024 + 514]
            s.wait_ge(sem("p4m1"), r + 1)
            if r >= 3:
                s.wait_ge(sem("p4si"), r - 2)
            s.activation(out=u_sb[:, ub * 512:(ub + 1) * 512], in_=ps[:, 0:512],
                         func=AF.Identity, scale=cws[:, jc * 3 + 0:jc * 3 + 1],
                         bias=0.0).then_inc(sem("p4c1"), 1)
            inc("p4c1")

        def silu_one(tb, jc):
            r = tb * NJC + jc
            ub = r % 3
            s.wait_ge(sem("p4cv"), 2 * r + 2)
            if tb >= 1:
                s.wait_ge(sem("p4q"), (tb - 1) * NJC + jc + 1)
            s.activation(out=h_v[:, jc, 0:512],
                         in_=u_sb[:, ub * 512:(ub + 1) * 512],
                         func=AF.Silu, bias=cbc[:, jc:jc + 1]).then_inc(sem("p4si"), 1)
            inc("p4si")

        def round_one(tb, jc):
            r = tb * NJC + jc
            qs = r % 2
            s.wait_ge(sem("p4qa"), r + 1)
            if r >= 1:
                s.wait_ge(sem("p4q"), r)
            s.activation(out=qscr2[:],
                         in_=qscr[:, qs * 512:(qs + 1) * 512],
                         func=AF.Identity, bias=meff[:, 0:1]).then_inc(sem("p4qb"), 1)
            inc("p4qb")
            s.wait_ge(sem("p4qb"), val("p4qb"))
            s.activation(out=hq_v[:, jc, 0:512],
                         in_=qscr2[:],
                         func=AF.Identity, bias=meffneg[:, 0:1]).then_inc(sem("p4q"), 1)
            inc("p4q")

        def silu_block(tb):
            for jc in range(NJC):
                op1_one(tb, jc)
                if jc >= 2:
                    silu_one(tb, jc - 2)
                if tb >= 1:
                    round_one(tb - 1, jc)
            silu_one(tb, NJC - 2)
            silu_one(tb, NJC - 1)

        def w2q_block(tb):
            for co in range(2):
                for jc in range(NJC):
                    rr = (tb * 2 + co) * NJC + jc
                    b2 = rr % 4
                    if b2 >= 2:
                        continue
                    if rr >= 4:
                        s.wait_ge(sem("p4m2c"), rr - 3)
                    s.dma_start(out=w2q_sb[:, b2 * 512:(b2 + 1) * 512],
                                in_=w2q_d[jc * 128:(jc + 1) * 128,
                                          co * 512:(co + 1) * 512]).then_inc(
                                    sem(f"p4w2l{b2}"), 16)
                    inc(f"p4w2l{b2}", 16)

        silu_block(0)
        for tb in range(1, NTB):
            silu_block(tb)
            w2q_block(tb - 1)
        for jc in range(NJC):
            round_one(NTB - 1, jc)
            for co in range(2):
                idx = 2 * jc + co
                b2 = idx % 4
                if b2 >= 2:
                    continue
                s.wait_ge(sem("p4m2c"), 192 + idx - 3)
                s.dma_start(out=w2q_sb[:, b2 * 512:(b2 + 1) * 512],
                            in_=w2q_d[jc * 128:(jc + 1) * 128,
                                      co * 512:(co + 1) * 512]).then_inc(
                                sem(f"p4w2l{b2}"), 16)
                inc(f"p4w2l{b2}", 16)
    on("scalar", p4_scalar)

    def p4_tensor(t):
        t.wait_ge(sem("p1z"), 2)
        t.wait_ge(sem("p35oev"), P35OEV)

        def mm1_block(tb, jc0, jc1):
            for jc in range(jc0, jc1):
                r = tb * NJC + jc
                b = r % 2
                pslot = jc % 2
                ps = psA[:, pslot * 1024:pslot * 1024 + 514]
                t.wait_ge(sem(f"p4w1l{b}"), 16 * (r // 2 + 1))
                if r >= 2:
                    t.wait_ge(sem("p4cv"), 2 * (r - 1))
                for cc in range(NCH):
                    t.matmul(ps[:, 0:512],
                             w1q_sb[:, b * 1024 + cc * 128:b * 1024 + (cc + 1) * 128],
                             xqTv[:, cc, tb * 512:tb * 512 + 512],
                             start=(cc == 0), stop=(cc == NCH - 1))
                for cc in range(NCH):
                    mm = t.matmul(ps[:, 512:514],
                                  w1q_sb[:, b * 1024 + cc * 128:b * 1024 + (cc + 1) * 128],
                                  xqTv[:, cc, tb * 512 + 512:tb * 512 + 514],
                                  start=(cc == 0), stop=(cc == NCH - 1))
                mm.then_inc(sem("p4m1"), 1)
                inc("p4m1")

        def chain_block(tb):
            t.wait_ge(sem("p4sh"), 8 * tb + 1)
            for j in range(4):
                ao = AMX_OFF[j]
                mm = t.transpose(psA[:, ao:ao + 128],
                                 amax1[:, j * 128:(j + 1) * 128], identf[:])
            mm.then_inc(sem("p4sh"), 1)
            inc("p4sh")
            t.wait_ge(sem("p4sh"), 8 * tb + 4)
            for j in range(4):
                ao = AMX_OFF[j]
                mm = t.transpose(psA[0:1, ao:ao + 128],
                                 sh_cols[:, tb * 4 + j:tb * 4 + j + 1], identf[:])
            mm.then_inc(sem("p4sh"), 1)
            inc("p4sh")
            t.wait_ge(sem("p4sh"), 8 * tb + 6)
            for j in range(4):
                ao = AMX_OFF[j]
                mm = t.matmul(psA[:, ao:ao + 128],
                              onesr[0:1, 0:128], trow[0:1, j * 128:(j + 1) * 128],
                              start=True, stop=True)
            mm.then_inc(sem("p4sh"), 1)
            inc("p4sh")

        def mm2_block(tb):
            for co in range(2):
                for jc in range(NJC):
                    rr = (tb * 2 + co) * NJC + jc
                    b2 = rr % 4
                    t.wait_ge(sem(f"p4w2l{b2}"), 16 * (rr // 4 + 1))
                    t.wait_ge(sem("p4q"), tb * NJC + jc + 1)
                    if jc == 0:
                        t.wait_ge(sem("p4fin"), (tb * 2 + co) * 4)
                    for tt in range(4):
                        psm2 = psD[:, 0:512] if tt == 3 else psC[:, tt * 512:(tt + 1) * 512]
                        mm = t.matmul(psm2,
                                      hq_v[:, jc, tt * 128:(tt + 1) * 128],
                                      w2q_sb[:, b2 * 512:(b2 + 1) * 512],
                                      start=(jc == 0), stop=(jc == NJC - 1))
                    mm.then_inc(sem("p4m2c"), 1)
                    inc("p4m2c")

        def mm2_block_last(tb):
            t.wait_ge(sem("p4sh"), 8 * tb + 8)
            t.wait_ge(sem("p4cv"), 2 * (tb + 1) * NJC)
            t.wait_ge(sem("p4fin"), tb * 2 * 4)
            for jc in range(NJC):
                t.wait_ge(sem("p4q"), tb * NJC + jc + 1)
                for co in range(2):
                    idx = 2 * jc + co
                    b2 = idx % 4
                    t.wait_ge(sem(f"p4w2l{b2}"), 16 * (48 + idx // 4 + 1))
                    for tt in range(4):
                        if co == 0:
                            psm2 = psD[:, 0:512] if tt == 3 else psC[:, tt * 512:(tt + 1) * 512]
                        else:
                            psm2 = psA[:, tt * 512:(tt + 1) * 512]
                        mm = t.matmul(psm2,
                                      hq_v[:, jc, tt * 128:(tt + 1) * 128],
                                      w2q_sb[:, b2 * 512:(b2 + 1) * 512],
                                      start=(jc == 0), stop=(jc == NJC - 1))
                    mm.then_inc(sem("p4m2c"), 1)
                    inc("p4m2c")

        mm1_block(0, 0, NJC)
        for tb in range(1, NTB):
            mm1_block(tb, 0, 2)
            chain_block(tb - 1)
            mm1_block(tb, 2, NJC)
            mm2_block(tb - 1)
        chain_block(NTB - 1)
        mm2_block_last(NTB - 1)
    on("tensor", p4_tensor)

    def p4_vec(v):
        v.wait_ge(sem("wmv"), WSC)

        def chain_v_block(tb):
            v.wait_ge(sem("p4sh"), 8 * tb + 2)
            for j in range(4):
                ti = tb * 4 + j
                ao = AMX_OFF[j]
                v.reduce_max(tiny[:, 9:10], psA[:, ao:ao + 128],
                             axis=AX).then_inc(sem("p4v"), 1); inc("p4v")
                v.wait_ge(sem("p4v"), val("p4v"))
                v.tensor_scalar(tiny[:, 10:11], tiny[:, 9:10], 1e-5, None,
                                OP.max).then_inc(sem("p4v"), 1); inc("p4v")
                v.wait_ge(sem("p4v"), val("p4v"))
                v.tensor_scalar(tiny[:, 11:12], tiny[:, 10:11], 1.0 / 127.0, None,
                                OP.mult).then_inc(sem("p4v"), 1); inc("p4v")
                v.wait_ge(sem("p4v"), val("p4v"))
                v.tensor_tensor(shc[:, ti:ti + 1], tiny[:, 11:12], mw2c[:],
                                OP.mult).then_inc(sem("p4v"), 1); inc("p4v")
                v.reciprocal(tiny[:, 12:13], tiny[:, 10:11]).then_inc(sem("p4v"), 1); inc("p4v")
                v.wait_ge(sem("p4v"), val("p4v"))
                v.tensor_scalar(sh_cols[:, ti:ti + 1], tiny[:, 12:13], 127.0, None,
                                OP.mult).then_inc(sem("p4v"), 1); inc("p4v")
            v.wait_ge(sem("p4v"), val("p4v"))
            v.engine_nop().then_inc(sem("p4sh"), 1)
            inc("p4sh")
            v.engine_nop().then_inc(sem("p4sh"), 1)
            inc("p4sh")
            v.wait_ge(sem("p4sh"), 8 * tb + 5)
            v.tensor_copy(trow[0:1, 0:256], psA[0:1, 640:896])
            v.tensor_copy(trow[0:1, 256:512], psA[0:1, 1664:1920]).then_inc(sem("p4sh"), 1)
            inc("p4sh")
            v.wait_ge(sem("p4sh"), 8 * tb + 7)
            v.tensor_copy(shb_sb[:, 0:256], psA[:, 640:896])
            v.tensor_copy(shb_sb[:, 256:512], psA[:, 1664:1920]).then_inc(sem("p4sh"), 1)
            inc("p4sh")

        def finals_co(tb, co):
            if True:
                k_ = tb * 2 + co
                if tb == NTB - 1:
                    v.wait_ge(sem("p4m2c"), 192 + 63 + co)
                else:
                    v.wait_ge(sem("p4m2c"), (k_ + 1) * NJC)
                for tt in range(4):
                    ti = tb * 4 + tt
                    osl = out_t[:, (tt % 2) * 1024 + co * 512:(tt % 2) * 1024 + (co + 1) * 512]
                    v.wait_ge(sem(f"p4xl{tt}"), 16 * (k_ + 1))
                    v.wait_ge(sem("p35oev"), P35OEV)
                    s_cls = (tt % 2) * 2 + co
                    m_ = k_ // 2
                    if tt >= 2:
                        v.wait_ge(sem(f"p4st{s_cls}"), 16 * (2 * m_ + 1))
                    elif m_ >= 1:
                        v.wait_ge(sem(f"p4st{s_cls}"), 16 * (2 * m_))
                    v.tensor_tensor(
                        osl,
                        xr_t[:, tt * 512:(tt + 1) * 512],
                        y_av[:, ti, co * 512:(co + 1) * 512],
                        OP.add).then_inc(sem("p4f1"), 1); inc("p4f1")
                    v.wait_ge(sem("p4f1"), val("p4f1"))
                    if tb == NTB - 1 and co == 1:
                        psff = psA[:, tt * 512:(tt + 1) * 512]
                    elif tt == 3:
                        psff = psD[:, 0:512]
                    else:
                        psff = psC[:, tt * 512:(tt + 1) * 512]
                    v.scalar_tensor_tensor(
                        osl,
                        psff,
                        shc[:, ti:ti + 1],
                        osl,
                        OP.mult, OP.add).then_inc(sem("p4fin"), 1); inc("p4fin")

        hT = h_act.ap().rearrange("p (j t) -> p t j", j=NJC)

        def amax_piece(tb, k):
            v.wait_ge(sem("p4si"), tb * NJC + 4 * k + 4)
            v.wait_ge(sem("p4fo"), val("p4fo"))
            if k == 0:
                if tb >= 1:
                    v.wait_ge(sem("p4sh"), 8 * (tb - 1) + 2)
                v.reduce_max(amax1[:], hT[:, :, 0:4], axis=AX,
                             apply_absolute_value=True).then_inc(sem("p4fo"), 1)
                inc("p4fo")
                return
            v.reduce_max(amax2[:], hT[:, :, 4 * k:4 * k + 4], axis=AX,
                         apply_absolute_value=True).then_inc(sem("p4fo"), 1)
            inc("p4fo")
            v.wait_ge(sem("p4fo"), val("p4fo"))
            mm = v.tensor_max(amax1[:], amax1[:], amax2[:])
            if k == 7:
                mm.then_inc(sem("p4sh"), 1)
                inc("p4sh")
            else:
                mm.then_inc(sem("p4fo"), 1)
                inc("p4fo")

        def conv_one(tb, jc):
            r = tb * NJC + jc
            pslot = jc % 2
            ub = r % 3
            ps = psA[:, pslot * 1024:pslot * 1024 + 514]
            v.wait_ge(sem("p4m1"), r + 1)
            v.wait_ge(sem("p4c1"), r + 1)
            v.scalar_tensor_tensor(u_sb[:, ub * 512:(ub + 1) * 512],
                                   ps[:, 1:513], cws[:, jc * 3 + 1:jc * 3 + 2],
                                   u_sb[:, ub * 512:(ub + 1) * 512],
                                   OP.mult, OP.add).then_inc(sem("p4cv"), 1); inc("p4cv")
            v.wait_ge(sem("p4cv"), val("p4cv"))
            v.scalar_tensor_tensor(u_sb[:, ub * 512:(ub + 1) * 512],
                                   ps[:, 2:514], cws[:, jc * 3 + 2:jc * 3 + 3],
                                   u_sb[:, ub * 512:(ub + 1) * 512],
                                   OP.mult, OP.add).then_inc(sem("p4cv"), 1); inc("p4cv")

        def quant_one(tb, jc):
            r = tb * NJC + jc
            qs = r % 2
            v.wait_ge(sem("p4sh"), 8 * tb + 8)
            if r >= 2:
                v.wait_ge(sem("p4qb"), r - 1)
            v.tensor_tensor(qscr[:, qs * 512:(qs + 1) * 512], h_v[:, jc, 0:512],
                            shb_sb[:], OP.mult).then_inc(sem("p4qa"), 1); inc("p4qa")

        for jc in range(NJC):
            conv_one(0, jc)
            if jc >= 6 and (jc - 6) % 4 == 0:
                amax_piece(0, (jc - 6) // 4)
        amax_piece(0, 7)
        for tb in range(1, NTB):
            if tb >= 2:
                finals_co(tb - 2, 1)
            chain_v_block(tb - 1)
            for jc in range(NJC):
                quant_one(tb - 1, jc)
                conv_one(tb, jc)
                if jc >= 6 and (jc - 6) % 4 == 0:
                    amax_piece(tb, (jc - 6) // 4)
            amax_piece(tb, 7)
            finals_co(tb - 1, 0)
        finals_co(NTB - 2, 1)
        chain_v_block(NTB - 1)
        for jc in range(NJC):
            quant_one(NTB - 1, jc)
        finals_co(NTB - 1, 0)
        finals_co(NTB - 1, 1)
    on("vector", p4_vec)

    def p4_gsync(g):
        g.wait_ge(sem("wqst0"), WQST0)
        g.wait_ge(sem("wqst1"), WQST1)

        def w1q_block(tb):
            for jc in range(NJC):
                r = tb * NJC + jc
                b = r % 2
                if r >= 2:
                    g.wait_ge(sem("p4m1"), r - 1)
                g.dma_start(out=w1q_sb[:, b * 1024:(b + 1) * 1024],
                            in_=w1q_src(jc)).then_inc(sem(f"p4w1l{b}"), 16)
                inc(f"p4w1l{b}", 16)

        def x_block(k_):
            co = k_ % 2
            tb = k_ // 2
            if k_ == 0:
                g.wait_ge(sem("p35o"), P35OT)
            for tt in range(4):
                ti = tb * 4 + tt
                if k_ >= 1:
                    g.wait_ge(sem("p4f1"), (k_ - 1) * 4 + tt + 1)
                g.dma_start(out=xr_t[:, tt * 512:(tt + 1) * 512],
                            in_=x_d[ti * 128:(ti + 1) * 128,
                                    co * 512:(co + 1) * 512]).then_inc(sem(f"p4xl{tt}"), 16)
                inc(f"p4xl{tt}", 16)

        w1q_block(0)
        x_block(0)
        w1q_block(1)
        x_block(1)
        w1q_block(2)
        x_block(2)
        x_block(3)
        w1q_block(3)
        x_block(4)
        x_block(5)
        x_block(6)
        x_block(7)
    on("gpsimd", p4_gsync)

    def p4_sync(q):
        q.wait_ge(sem("wqst0"), WQST0)
        q.wait_ge(sem("wqst1"), WQST1)

        def w2q_sloads(order):
            for rr, jc, co in order:
                b2 = rr % 4
                if b2 < 2:
                    continue
                if rr >= 4:
                    q.wait_ge(sem("p4m2c"), rr - 3)
                q.dma_start(out=w2q_sb[:, b2 * 512:(b2 + 1) * 512],
                            in_=w2q_d[jc * 128:(jc + 1) * 128,
                                      co * 512:(co + 1) * 512]).then_inc(
                                sem(f"p4w2l{b2}"), 16)
                inc(f"p4w2l{b2}", 16)

        def stores_co(tb, co):
            for tt in range(4):
                ti = tb * 4 + tt
                n = (tb * 2 + co) * 4 + tt
                q.wait_ge(sem("p4fin"), n + 1)
                s_cls = (tt % 2) * 2 + co
                q.dma_start(out=out_d[ti * 128:(ti + 1) * 128,
                                      co * 512:(co + 1) * 512],
                            in_=out_t[:, (tt % 2) * 1024 + co * 512:(tt % 2) * 1024 + (co + 1) * 512]
                            ).then_inc(sem(f"p4st{s_cls}"), 16)
                inc(f"p4st{s_cls}", 16)

        for tb in range(NTB):
            if tb < NTB - 1:
                w2q_sloads([((tb * 2) * NJC + jc, jc, 0) for jc in range(NJC)])
                stores_co(tb, 0)
                w2q_sloads([((tb * 2 + 1) * NJC + jc, jc, 1) for jc in range(NJC)])
                stores_co(tb, 1)
            else:
                w2q_sloads([(tb * 2 * NJC + 2 * jc + co, jc, co)
                            for jc in range(NJC) for co in range(2)])
                stores_co(tb, 0)
                stores_co(tb, 1)
        for s_cls in range(4):
            q.wait_ge(sem(f"p4st{s_cls}"), val(f"p4st{s_cls}"))
    on("sync", p4_sync)

    g_yat.__exit__(None, None, None)
    st.close()
    return nc


def colswizzle_test(w, nb, bw):
    wt = np.asarray(w, dtype=np.float32).T
    return np.ascontiguousarray(
        wt.reshape(8, 128, nb, bw).transpose(2, 1, 0, 3).reshape(nb, 128, 8 * bw))


_NC_CACHE = [None]


def kernel(L0_attn_x, L0_attn_wq_w, L0_attn_wq_b, L0_attn_wk_w, L0_attn_wk_b,
           L0_attn_wv_w, L0_attn_wv_b, L0_attn_wo_w, L0_attn_wo_b,
           L0_ffn_w1, L0_ffn_conv_w, L0_ffn_conv_b, L0_ffn_w2):
    f32 = lambda a: np.ascontiguousarray(np.asarray(a, dtype=np.float32))
    x = f32(L0_attn_x)

    def colswizzle(w, nb, bw):
        wt = np.asarray(w, dtype=np.float32).T
        return np.ascontiguousarray(
            wt.reshape(8, 128, nb, bw).transpose(2, 1, 0, 3).reshape(nb, 128, 8 * bw))

    wq_t = colswizzle(L0_attn_wq_w, 8, 128)
    wk_t = colswizzle(L0_attn_wk_w, 8, 128)
    wv_t = colswizzle(L0_attn_wv_w, 4, 256)
    wo_t = f32(np.asarray(L0_attn_wo_w).T)
    w1_t = f32(np.asarray(L0_ffn_w1).T)
    w2_t = f32(np.asarray(L0_ffn_w2).T)
    cw = f32(np.asarray(L0_ffn_conv_w).reshape(32, 128, 3).transpose(1, 0, 2).reshape(128, 96))
    cb = f32(np.asarray(L0_ffn_conv_b).reshape(32, 128).T)
    bq = f32(np.asarray(L0_attn_wq_b).reshape(8, 128).T)
    bk = f32(np.asarray(L0_attn_wk_b).reshape(8, 128).T)
    bv = f32(np.asarray(L0_attn_wv_b).reshape(1, C))
    bo = f32(np.asarray(L0_attn_wo_b).reshape(1, C))
    identf = np.eye(128, dtype=np.float32)
    identb = np.eye(128, dtype=ml_dtypes.bfloat16)
    onesr = np.ones((1, 128), dtype=np.float32)
    if _NC_CACHE[0] is None:
        _NC_CACHE[0] = _build()
    nc = _NC_CACHE[0]

    shared = dict(wq_t=wq_t, wk_t=wk_t, wv_t=wv_t, wo_t=wo_t, w1_t=w1_t,
                  w2_t=w2_t, cw=cw, cb=cb, bq=bq, bk=bk, bv=bv, bo=bo,
                  identf=identf, identb=identb, onesr=onesr)
    in_maps = [dict(shared, x=np.ascontiguousarray(x[b])) for b in range(8)]
    res = run_bass_kernel_spmd(nc, in_maps, core_ids=list(range(8)))
    out = np.stack([np.asarray(res.results[b]["out"], dtype=np.float32)
                    for b in range(8)], axis=0)
    return out
